# version 29
# baseline (speedup 1.0000x reference)
import numpy as np
import ml_dtypes
import concourse.bass as bass
import concourse.mybir as mybir
from concourse.bass_utils import run_bass_kernel_spmd
from contextlib import ExitStack

F32 = mybir.dt.float32
BF16 = mybir.dt.bfloat16
bf16 = ml_dtypes.bfloat16
AF = mybir.ActivationFunctionType
ALU = mybir.AluOpType
AX = mybir.AxisListType

ENGINES = ("tensor", "vector", "scalar", "gpsimd", "sync")

D = 2048
SEQ = 4096
TOWN = 2048
NH = 16
HF = 16384
HH = 8192
MSHIFT = -16.0
SCALE_QK = float(np.float32(1.0) / np.sqrt(np.float32(128.0)))
EPS_LN = 1e-5
EPS_RMS = float(np.finfo(np.float32).eps)


class S:
    def __init__(self, nc, es, name):
        self.h = es.enter_context(nc.semaphore(name))
        self.n = 0

    def bump(self, k):
        self.n += k
        return self.n


class Rec:
    def __init__(self):
        self.lists = {e: [] for e in ENGINES}

    def add(self, eng, fn):
        self.lists[eng].append(fn)

    def wait(self, eng, sem, val):
        if val <= 0:
            return
        self.add(eng, lambda q, s=sem.h, v=val: q.wait_ge(s, v))

    def op(self, eng, name, *args, sem=None, inc=1, **kw):
        tgt = sem.bump(inc) if sem is not None else None

        def f(q, nm=name, aa=args, kk=dict(kw), s=sem, ii=inc):
            inst = getattr(q, nm)(*aa, **kk)
            if s is not None:
                inst.then_inc(s.h, ii)

        self.add(eng, f)
        return tgt

    def mm(self, eng, out, lhsT, rhs, start, stop, sem=None, inc=1):
        tgt = sem.bump(inc) if sem is not None else None

        def f(q, o=out, a=lhsT, b=rhs, st=start, sp=stop, s=sem, ii=inc):
            inst = q.matmul(o, a, b, start=st, stop=sp)
            if s is not None:
                inst.then_inc(s.h, ii)

        self.add(eng, f)
        return tgt


class Slot:

    def __init__(self, R, nc, es, name):
        self.R = R
        self.sem = S(nc, es, name)

    def dma(self, eng, out, in_, **kw):
        tgt = self.sem.bump(16)

        def f(q, o=out, i=in_, s=self.sem.h, k=dict(kw)):
            q.dma_start(out=o, in_=i, **k).then_inc(s, 16)

        self.R.add(eng, f)
        return tgt

    def wait(self, eng, tgt=None):
        self.R.wait(eng, self.sem, self.sem.n if tgt is None else tgt)


def finish(R, nc, es):
    with nc.Block() as block:
        @block.tensor
        def _(q):
            for f in R.lists["tensor"]:
                f(q)

        @block.vector
        def _(q):
            for f in R.lists["vector"]:
                f(q)

        @block.scalar
        def _(q):
            for f in R.lists["scalar"]:
                f(q)

        @block.gpsimd
        def _(q):
            for f in R.lists["gpsimd"]:
                f(q)

        @block.sync
        def _(q):
            for f in R.lists["sync"]:
                f(q)

    es.close()
    return nc


def build_program(dbg=(), upto=99):
    dbg = set(dbg)
    nc = bass.Bass("TRN2", target_bir_lowering=False, debug=False)

    def dram(name, shape, dtype, kind="Internal"):
        k = "ExternalOutput" if (name in dbg and kind == "Internal") else kind
        return nc.dram_tensor(name, shape, dtype, kind=k).ap()

    x_in = dram("x_in", [SEQ, D], F32, "ExternalInput")
    kb_in = dram("kb_in", [128, 2], F32, "ExternalInput")
    gb_in = dram("gb_in", [128], F32, "ExternalInput")
    q_w = dram("q_w", [D, D], F32, "ExternalInput")
    k_w = dram("k_w", [D, D], F32, "ExternalInput")
    v_w = dram("v_w", [D, D], F32, "ExternalInput")
    o_w = dram("o_w", [D, D], F32, "ExternalInput")
    gate_w = dram("gate_w", [128, 128], F32, "ExternalInput")
    w_gu = dram("w_gu", [2048, D], F32, "ExternalInput")
    w_pr = dram("w_pr", [256, HH], F32, "ExternalInput")
    out = dram("out", [TOWN, D], F32, "ExternalOutput")

    ln1T_s = dram("ln1T_s", [D, TOWN], BF16)
    wTq_s = dram("wTq_s", [D, D], BF16)
    wTk_s = dram("wTk_s", [D, D], BF16)
    wTv_s = dram("wTv_s", [D, D], BF16)
    wTo_s = dram("wTo_s", [D, D], BF16)
    wTg_s = dram("wTg_s", [128, 128], BF16)
    k_own = dram("k_own", [NH * 128, TOWN], BF16)
    k_pair = dram("k_pair", [2 * NH * 128, TOWN], BF16)
    v_own = dram("v_own", [TOWN, D], BF16)
    v_pair = dram("v_pair", [2 * TOWN, D], BF16)
    g_s = dram("g_s", [NH, 128, TOWN], BF16)
    attn_s = dram("attn_s", [TOWN, D], BF16)
    x1_s = dram("x1_s", [TOWN, D], F32)
    wqgT_loc = dram("wqgT_loc", [D, 2048], BF16)
    wqpT_loc = dram("wqpT_loc", [HH, 256], BF16)
    wqgT_all = dram("wqgT_all", [8 * D, 2048], BF16)
    wqpT_all = dram("wqpT_all", [8 * HH, 256], BF16)
    h_s = dram("h_s", [HH, TOWN], BF16)
    f_s = dram("f_s", [D, TOWN], BF16)

    es = ExitStack()

    def sbuf(name, shape, dtype):
        return es.enter_context(nc.sbuf_tensor(name, shape, dtype))

    AR_W = sbuf("AR_W", [128, 16384], BF16)
    AR_ACT = sbuf("AR_ACT", [128, 32768], BF16)
    AR_GAV = sbuf("AR_GAV", [128, 32768], BF16)
    AR_D = sbuf("AR_D", [128, 8192], BF16)
    AR_E = sbuf("AR_E", [128, 2048], BF16)
    AR_F = sbuf("AR_F", [128, 5120], F32)
    STAT = sbuf("STAT", [128, 64], F32)
    QST = sbuf("QST", [128, 128], F32)
    KB = sbuf("KB", [128, 2], F32)
    GB = sbuf("GB", [128, 1], F32)
    EPSL = sbuf("EPSL", [128, 1], F32)
    EPSR = sbuf("EPSR", [128, 1], F32)
    ONES = sbuf("ONES", [128, 128], BF16)
    IDN = sbuf("IDN", [128, 128], BF16)
    MASKS = sbuf("MASKS", [128, 4 * 512], BF16)
    SCB = sbuf("SCB", [128, 32], BF16)

    RRMS = AR_F[:, 0:2048]
    psum = es.enter_context(nc.psum_tensor("psAll", [128, 4096], F32))

    def PS(b, n=1):
        return psum[:, b * 512:(b + n) * 512]

    R = Rec()
    sems = {k: S(nc, es, k) for k in ("bar", "sMM", "sEV", "sEX", "sPO")}
    bar, sMM, sEV, sEX, sPO = (sems[k] for k in
                               ("bar", "sMM", "sEV", "sEX", "sPO"))
    ccg = S(nc, es, "ccg")
    ccp = S(nc, es, "ccp")
    cck = [S(nc, es, f"cck{i}") for i in range(4)]
    ccv = [S(nc, es, f"ccv{i}") for i in range(8)]

    def SL(name):
        return Slot(R, nc, es, name)

    sl_kb = SL("l_kb")
    sl_xa = SL("l_xa")
    sl_xb = SL("l_xb")
    sl_wc = [SL("l_wc0"), SL("l_wc1")]
    sl_tt = [SL("l_tt0"), SL("l_tt1")]
    sl_wh = SL("l_wh")
    sl_K = [SL("l_K0"), SL("l_K1")]
    sl_V = [SL("l_V0"), SL("l_V1")]
    sl_G = [SL("l_G0"), SL("l_G1")]
    sl_hx = SL("l_hx")
    sl_wp = [SL("l_wp0"), SL("l_wp1")]
    sl_qs = [SL("l_qs0"), SL("l_qs1")]
    sl_ft = SL("l_ft")
    sl_wq2 = SL("l_wq2")
    sl_wt = [SL("l_wt0"), SL("l_wt1")]
    st_wt = [SL("s_wt0"), SL("s_wt1")]
    st_ln1 = [SL("s_ln10"), SL("s_ln11")]
    st_wc = [SL("s_wc0"), SL("s_wc1")]
    st_e = [SL("s_e0"), SL("s_e1"), SL("s_e2"), SL("s_e3")]
    st_wq = SL("s_wq")
    st_x1 = [SL("s_x10"), SL("s_x11")]
    st_ln2 = SL("s_ln2")
    st_h = [SL("s_h0"), SL("s_h1")]
    st_f = [SL("s_f0"), SL("s_f1"), SL("s_f2"), SL("s_f3")]
    st_out = [SL("s_out0"), SL("s_out1")]

    def barrier():
        for e in ENGINES:
            R.add(e, lambda q, s=bar.h: q.nop().then_inc(s, 1))
        bar.n += 5
        for e in ENGINES:
            R.wait(e, bar, bar.n)

    def ln_stats(src, sb, sq_dst):
        s1 = STAT[:, sb + 0:sb + 1]
        s2 = STAT[:, sb + 1:sb + 2]
        negm = STAT[:, sb + 2:sb + 3]
        m2t = STAT[:, sb + 3:sb + 4]
        varr = STAT[:, sb + 4:sb + 5]
        rstd = STAT[:, sb + 5:sb + 6]
        R.op("vector", "scalar_tensor_tensor", sq_dst, src, 1.0, src,
             op0=ALU.mult, op1=ALU.mult, accum_out=s2, sem=sEV)
        R.op("vector", "reduce_sum", s1, src, axis=AX.X, sem=sEV)
        R.wait("vector", sEV, sEV.n)
        R.op("vector", "tensor_scalar", negm, s1, -1.0 / D, None,
             op0=ALU.mult, sem=sEV)
        R.wait("vector", sEV, sEV.n)
        R.op("vector", "tensor_scalar", m2t, negm, negm, None,
             op0=ALU.mult, sem=sEV)
        R.wait("vector", sEV, sEV.n)
        R.op("vector", "scalar_tensor_tensor", varr, s2, 1.0 / D, m2t,
             op0=ALU.mult, op1=ALU.subtract, sem=sEV)
        R.wait("scalar", sEV, sEV.n)
        R.op("scalar", "activation", rstd, varr, AF.Ln, bias=EPSL[:],
             sem=sEX)
        R.wait("scalar", sEX, sEX.n)
        R.op("scalar", "activation", rstd, rstd, AF.Exp, scale=-0.5, sem=sEX)
        return sEX.n

    def ln_fin(src, dst, sb, rstd_t):
        R.wait("vector", sEX, rstd_t)
        R.op("vector", "tensor_scalar", dst, src,
             STAT[:, sb + 2:sb + 3], STAT[:, sb + 5:sb + 6],
             op0=ALU.add, op1=ALU.mult, sem=sEV)
        return sEV.n

    q_ev = {}
    q_st = {}
    q_tmm = {}

    def quant_block(j, src, dst):
        G = 32
        A = AR_GAV[:, 8192 + (j % 2) * 2048: 8192 + (j % 2) * 2048 + 2048]
        B = AR_GAV[:, (j % 2) * 4096: (j % 2) * 4096 + 2048]
        C = AR_GAV[:, (j % 2) * 4096 + 2048: (j % 2) * 4096 + 4096]
        gsumt = QST[:, 0:G]
        sct = QST[:, G:2 * G]
        hst = QST[:, 2 * G:3 * G]
        nhst = QST[:, 3 * G:4 * G]
        scb = SCB[:, 0:G]
        if j >= 2:
            R.wait("gpsimd", sEV, q_ev[j - 2])
        ca_t = sl_qs[j % 2].dma("gpsimd", A, src)
        R.wait("vector", sl_qs[j % 2].sem, ca_t)
        R.op("vector", "tensor_reduce", gsumt,
             A.rearrange("p (g k) -> p g k", k=64), axis=AX.X, op=ALU.add,
             apply_absolute_value=True, sem=sEV)
        R.wait("vector", sEV, sEV.n)
        R.op("vector", "tensor_scalar", scb, gsumt, 1.0 / 64, 1e-8,
             op0=ALU.mult, op1=ALU.max, sem=sEV)
        R.wait("vector", sEV, sEV.n)
        R.op("vector", "tensor_copy", sct, scb, sem=sEV)
        R.wait("vector", sEV, sEV.n)
        R.op("vector", "tensor_scalar", hst, sct, 0.5, None, op0=ALU.mult,
             sem=sEV)
        R.op("vector", "tensor_scalar", nhst, sct, -0.5, None, op0=ALU.mult,
             sem=sEV)
        R.wait("vector", sEV, sEV.n)
        if j >= 2:
            R.wait("vector", sMM, q_tmm[j - 2])
        R.op("vector", "tensor_tensor",
             B.rearrange("p (g k) -> p g k", k=64),
             A.rearrange("p (g k) -> p g k", k=64),
             hst[:, :, None].broadcast_to([128, G, 64]), op=ALU.is_gt,
             sem=sEV)
        R.op("vector", "tensor_tensor",
             C.rearrange("p (g k) -> p g k", k=64),
             A.rearrange("p (g k) -> p g k", k=64),
             nhst[:, :, None].broadcast_to([128, G, 64]), op=ALU.is_lt,
             sem=sEV)
        ev_m = sEV.n
        R.wait("gpsimd", sEV, ev_m)
        R.op("gpsimd", "tensor_tensor", A, B, C, op=ALU.subtract, sem=sPO)
        R.wait("vector", sPO, sPO.n)
        R.op("vector", "tensor_tensor",
             B.rearrange("p (g k) -> p g k", k=64),
             A.rearrange("p (g k) -> p g k", k=64),
             scb[:, :, None].broadcast_to([128, G, 64]), op=ALU.mult, sem=sEV)
        q_ev[j] = sEV.n
        R.wait("tensor", sEV, q_ev[j])
        q_tmm[j], q_st[j] = pe_T(B, 2048, dst)
        return q_st[j]

    qjobs = []
    for b in range(16):
        qjobs.append((w_gu[b * 128:(b + 1) * 128, :],
                      wqgT_loc[0:2048, b * 128:(b + 1) * 128]))
    for rb in range(2):
        for cb in range(4):
            qjobs.append((w_pr[rb * 128:(rb + 1) * 128,
                               cb * 2048:(cb + 1) * 2048],
                          wqpT_loc[cb * 2048:(cb + 1) * 2048,
                                   rb * 128:(rb + 1) * 128]))

    sl_kb.dma("sync", KB[:], kb_in[:])
    sl_kb.dma("sync", GB[:], gb_in[:, None])
    R.op("gpsimd", "memset", ONES[:], 1.0, sem=sPO)
    R.op("gpsimd", "memset", IDN[:], 1.0, sem=sPO)
    R.op("gpsimd", "memset", MASKS[:], 1.0, sem=sPO)
    R.wait("gpsimd", sPO, sPO.n)
    for j in range(4):
        R.op("gpsimd", "affine_select", MASKS[:, j * 512:(j + 1) * 512],
             MASKS[:, j * 512:(j + 1) * 512], pattern=[[1, 512]],
             compare_op=ALU.is_ge, fill=0.0, base=-128 * j,
             channel_multiplier=-1, sem=sPO)
    R.op("gpsimd", "affine_select", IDN[:], IDN[:], pattern=[[1, 128]],
         compare_op=ALU.is_ge, fill=0.0, base=0,
         channel_multiplier=-1, sem=sPO)
    R.wait("gpsimd", sPO, sPO.n)
    R.op("gpsimd", "affine_select", IDN[:], IDN[:], pattern=[[-1, 128]],
         compare_op=ALU.is_ge, fill=0.0, base=0,
         channel_multiplier=1, sem=sPO)
    idn_tgt = sPO.n
    R.wait("tensor", sPO, idn_tgt)
    R.op("vector", "memset", EPSL[:], EPS_LN, sem=sEV)
    R.op("vector", "memset", EPSR[:], EPS_RMS, sem=sEV)

    pt_state = {"j": 0, "ev": {}, "st": {}, "mm": {}, "slot": {}}

    def pe_T(src, ncol, dst_col, tsg=None, lag=2):
        j = pt_state["j"]
        pt_state["j"] += 1
        par = j % 2
        nt = ncol // 128
        PT = psum[:, 2048 + par * 1024:
                  2048 + (par + 1) * 1024].bitcast(BF16)
        TSG = (AR_GAV[:, 12288 + par * 2048: 12288 + par * 2048 + 2048]
               if tsg is None else tsg)
        if j >= 2:
            R.wait("tensor", sEX, pt_state["ev"][j - 2])
        for k in range(nt):
            R.op("tensor", "transpose", PT[:, k * 128:(k + 1) * 128],
                 src[:, k * 128:(k + 1) * 128], IDN[:],
                 sem=sMM if k == nt - 1 else None)
        pt_state["mm"][j] = sMM.n
        R.wait("scalar", sMM, pt_state["mm"][j])
        if j >= lag and (j - lag) in pt_state["st"]:
            stp = pt_state["slot"][j - lag]
            R.wait("scalar", stp.sem, pt_state["st"][j - lag])
        R.op("scalar", "activation", TSG[:, 0:ncol], PT[:, 0:ncol], AF.Copy,
             sem=sEX)
        pt_state["ev"][j] = sEX.n
        R.wait("scalar", sEX, pt_state["ev"][j])
        for c0 in range(0, ncol, 1024):
            seg = min(1024, ncol - c0)
            pt_state["st"][j] = st_wt[par].dma(
                "scalar",
                dst_col[c0:c0 + seg, :].rearrange("(k p) c -> p k c", p=128),
                TSG[:, c0:c0 + seg].rearrange("p (k c) -> p k c", c=128))
        pt_state["slot"][j] = st_wt[par]
        return pt_state["mm"][j], pt_state["st"][j]

    wcast = []
    for wsrc, wdstT in ((q_w, wTq_s), (k_w, wTk_s), (v_w, wTv_s),
                        (o_w, wTo_s)):
        for blk in range(16):
            wcast.append((wsrc[blk * 128:(blk + 1) * 128, :],
                          wdstT[0:2048, blk * 128:(blk + 1) * 128], 2048))
    wcast.append((gate_w[:, :], wTg_s[0:128, 0:128], 128))
    wc_n = [0]
    wc_tmm = {}

    def emit_wcast():
        j = wc_n[0]
        if j >= len(wcast):
            return
        wc_n[0] += 1
        src, dstT, ncol = wcast[j]
        par = j % 2
        stg = AR_D[:, par * 2048:par * 2048 + ncol]
        if j >= 2:
            R.wait("gpsimd", sMM, wc_tmm[j - 2])
        t = sl_wc[par].dma("gpsimd", stg, src)
        R.wait("tensor", sl_wc[par].sem, t)
        wc_tmm[j], _ = pe_T(stg, ncol, dstT)

    apply_hist = {}
    store_hist = {}
    xsl = [sl_xa, sl_xb]
    XP = [AR_ACT[:, 0:8192].bitcast(F32), AR_ACT[:, 8192:16384].bitcast(F32)]
    LOUT2 = [AR_ACT[:, 16384:20480], AR_ACT[:, 20480:24576]]
    SQ1 = AR_ACT[:, 24576:26624]

    q_store = {}
    lout_tmm = {}
    qj = [0]
    for p in range(8):
        par = p % 2
        xp = XP[par]
        c = par * 16
        if p >= 2:
            R.wait("sync", sEX, apply_hist[p - 2])
        for t in range(2):
            ld = xsl[par].dma(
                "sync", xp[:, t * 2048:(t + 1) * 2048],
                x_in[(16 + 2 * p + t) * 128:(16 + 2 * p + t + 1) * 128, :])
        for _ in range(9):
            emit_wcast()
        s2p = STAT[:, c:c + 2]
        s1p = STAT[:, c + 2:c + 4]
        ngp = STAT[:, c + 4:c + 6]
        m2p = STAT[:, c + 6:c + 8]
        vrp = STAT[:, c + 8:c + 10]
        rsp = STAT[:, c + 10:c + 12]
        R.wait("vector", xsl[par].sem, ld)
        for t in range(2):
            R.op("vector", "scalar_tensor_tensor",
                 AR_ACT[:, 24576 + t * 2048:26624 + t * 2048],
                 xp[:, t * 2048:(t + 1) * 2048], 1.0,
                 xp[:, t * 2048:(t + 1) * 2048],
                 op0=ALU.mult, op1=ALU.mult,
                 accum_out=s2p[:, t:t + 1], sem=sEV)
            R.op("vector", "reduce_sum", s1p[:, t:t + 1],
                 xp[:, t * 2048:(t + 1) * 2048], axis=AX.X, sem=sEV)
        R.wait("vector", sEV, sEV.n)
        R.op("vector", "tensor_scalar", ngp, s1p, -1.0 / D, None,
             op0=ALU.mult, sem=sEV)
        R.wait("vector", sEV, sEV.n)
        R.op("vector", "tensor_tensor", m2p, ngp, ngp, op=ALU.mult, sem=sEV)
        R.wait("vector", sEV, sEV.n)
        R.op("vector", "scalar_tensor_tensor", vrp, s2p, 1.0 / D, m2p,
             op0=ALU.mult, op1=ALU.subtract, sem=sEV)
        R.wait("scalar", sEV, sEV.n)
        R.op("scalar", "activation", rsp, vrp, AF.Ln, bias=EPSL[:], sem=sEX)
        R.wait("scalar", sEX, sEX.n)
        R.op("scalar", "activation", rsp, rsp, AF.Exp, scale=-0.5, sem=sEX)
        R.wait("vector", sEX, sEX.n)
        nmrp = STAT[:, c + 12:c + 14]
        if p >= 2:
            R.wait("vector", sEX, apply_hist[p - 2])
        R.op("vector", "tensor_tensor", nmrp, ngp, rsp, op=ALU.mult, sem=sEV)
        R.wait("scalar", sEV, sEV.n)
        if p >= 2:
            R.wait("scalar", sMM, lout_tmm[p - 2])
        for t in range(2):
            R.op("scalar", "activation", LOUT2[par][:, t * 2048:
                                                    (t + 1) * 2048],
                 xp[:, t * 2048:(t + 1) * 2048], AF.Identity,
                 scale=STAT[:, c + 10 + t:c + 11 + t],
                 bias=STAT[:, c + 12 + t:c + 13 + t], sem=sEX)
        apply_hist[p] = sEX.n
        R.wait("tensor", sEX, apply_hist[p])
        for t in range(2):
            lout_tmm[p], _ = pe_T(
                LOUT2[par][:, t * 2048:(t + 1) * 2048], 2048,
                ln1T_s[0:2048, (2 * p + t) * 128:(2 * p + t + 1) * 128])
        for _ in range(3):
            q_store[qj[0]] = quant_block(qj[0], *qjobs[qj[0]])
            qj[0] += 1
    while wc_n[0] < len(wcast):
        emit_wcast()
    barrier()
    if upto < 2:
        return finish(R, nc, es)


    def emit_gather(in_ap, out_ap, sem, groups):
        def f(q, s=sem.h, a=in_ap, b=out_ap, g=groups):
            q.collective_compute(
                "AllGather", ALU.bypass,
                replica_groups=g,
                ins=[a], outs=[b]).then_inc(s)

        R.add("gpsimd", f)
        sem.n = 1

    PAIRS = [[0, 1], [2, 3], [4, 5], [6, 7]]
    ALL8 = [[0, 1, 2, 3, 4, 5, 6, 7]]

    def proj_wstat(w_scratch, kind, tokens_lo, n_tt):
        evac_hist = []
        mm_hist = []
        st_wt[0].wait("sync")
        st_wt[1].wait("sync")
        st_wt[0].wait("vector")
        st_wt[1].wait("vector")
        TTBUF = [AR_D[:, 0:8192], AR_GAV[:, 0:8192]]
        WHB = [AR_W, AR_GAV[:, 16384:32768]]

        def tt_read(j):
            sp = j % 2
            for k in range(16):
                sl_tt[sp].dma(
                    "sync", TTBUF[sp][:, k * 512:(k + 1) * 512],
                    ln1T_s[k * 128:(k + 1) * 128,
                           tokens_lo + j * 512: tokens_lo + (j + 1) * 512])

        wt_last = 0
        for eh in range(2):
            for k in range(16):
                wt_last = sl_wh.dma(
                    "sync", WHB[eh][:, k * 1024:(k + 1) * 1024],
                    w_scratch[k * 128:(k + 1) * 128,
                              eh * 1024:(eh + 1) * 1024])
        tt_read(0)
        for tt in range(n_tt):
            if tt + 1 < n_tt:
                if mm_hist:
                    R.wait("sync", sMM, mm_hist[-1])
                tt_read(tt + 1)
            sl_tt[tt % 2].wait("tensor")
            R.wait("tensor", sl_wh.sem, wt_last)
            for eb in range(16):
                eh, e = eb // 8, eb % 8
                idx = len(evac_hist)
                if idx >= 4:
                    R.wait("tensor", sEV, evac_hist[idx - 4])
                pb = PS(idx % 4)
                for k in range(16):
                    R.mm("tensor", pb,
                         WHB[eh][:, k * 1024 + e * 128:
                                 k * 1024 + (e + 1) * 128],
                         TTBUF[tt % 2][:, k * 512:(k + 1) * 512],
                         start=(k == 0), stop=(k == 15),
                         sem=sMM if k == 15 else None)
                mm_hist.append(sMM.n)
                R.wait("vector", sMM, sMM.n)
                if kind == "Q":
                    dst = AR_ACT[:, eb * 2048 + tt * 512:
                                 eb * 2048 + (tt + 1) * 512]
                    R.op("vector", "tensor_copy", dst, pb, sem=sEV)
                    evac_hist.append(sEV.n)
                else:
                    sl = idx % 4
                    stg = AR_E[:, sl * 512:(sl + 1) * 512]
                    st_e[sl].wait("vector")
                    R.op("vector", "tensor_copy", stg, pb, sem=sEV)
                    evac_hist.append(sEV.n)
                    R.wait("sync", sEV, sEV.n)
                    st_e[sl].dma("sync",
                                 k_own[eb * 128:(eb + 1) * 128,
                                       tt * 512:(tt + 1) * 512], stg)
        return mm_hist[-1]

    q_last_mm = proj_wstat(wTq_s, "Q", 0, 4)
    q_done_ev = sEV.n

    R.wait("sync", sMM, q_last_mm)
    gwT_t = sl_wh.dma("sync", AR_W[:, 0:128], wTg_s[:, :])
    R.wait("tensor", sl_wh.sem, gwT_t)
    R.wait("tensor", sEV, q_done_ev)
    sl_kb.wait("scalar")
    g_sig = []
    for h in range(NH):
        for tt in range(4):
            i = h * 4 + tt
            pb = PS(i % 4)
            if i >= 4:
                R.wait("tensor", sEX, g_sig[i - 4])
            R.mm("tensor", pb, AR_W[:, 0:128],
                 AR_ACT[:, h * 2048 + tt * 512: h * 2048 + (tt + 1) * 512],
                 start=True, stop=True, sem=sMM)
            R.wait("scalar", sMM, sMM.n)
            sl = i % 4
            stg = AR_E[:, sl * 512:(sl + 1) * 512]
            st_e[sl].wait("scalar")
            R.op("scalar", "activation", stg, pb, AF.Sigmoid, bias=GB[:],
                 sem=sEX)
            g_sig.append(sEX.n)
            R.wait("sync", sEX, sEX.n)
            st_e[sl].dma("sync", g_s[h, :, tt * 512:(tt + 1) * 512], stg)
    barrier()
    if upto < 3:
        return finish(R, nc, es)

    proj_wstat(wTk_s, "K", 0, 4)
    for sl4 in st_e:
        sl4.wait("gpsimd")
    for ch in range(4):
        emit_gather(k_own[ch * 512:(ch + 1) * 512, :].opt(),
                    k_pair[ch * 1024:(ch + 1) * 1024, :].opt(),
                    cck[ch], PAIRS)
    barrier()
    if upto < 4:
        return finish(R, nc, es)

    v_evac = []
    v_mm = []
    VTB = [AR_D[:, 0:8192], AR_GAV[:, 0:8192]]
    VWB = [AR_W, AR_GAV[:, 16384:32768]]

    def v_tt_read(j):
        sp = j % 2
        for k in range(16):
            sl_tt[sp].dma("sync", VTB[sp][:, k * 512:(k + 1) * 512],
                          ln1T_s[k * 128:(k + 1) * 128,
                                 j * 512:(j + 1) * 512])

    vt_last = 0
    for eh in range(2):
        for k in range(16):
            vt_last = sl_wh.dma(
                "sync", VWB[eh][:, k * 1024:(k + 1) * 1024],
                wTv_s[k * 128:(k + 1) * 128,
                      eh * 1024:(eh + 1) * 1024])
    v_gt = {}
    v_tt_read(0)
    for tb in range(16):
        tt = tb // 4
        if tb % 2 == 0 and tb >= 2:
            ch = tb // 2 - 1
            for sl4 in st_e:
                sl4.wait("gpsimd")
            emit_gather(v_own[ch * 256:(ch + 1) * 256, :].opt(),
                        v_pair[ch * 512:(ch + 1) * 512, :].opt(),
                        ccv[ch], PAIRS)
        if tb % 4 == 0:
            if tt + 1 < 4:
                if v_mm:
                    R.wait("sync", sMM, v_mm[-1])
                v_tt_read(tt + 1)
            sl_tt[tt % 2].wait("tensor")
            R.wait("tensor", sl_wh.sem, vt_last)
        for eog in range(4):
            eh, eo = eog // 2, eog % 2
            idx = len(v_evac)
            if idx >= 4:
                R.wait("tensor", sEV, v_evac[idx - 4])
            pb = PS(idx % 4)
            for k in range(16):
                R.mm("tensor", pb,
                     VTB[tt % 2][:, k * 512 + (tb % 4) * 128:
                                 k * 512 + (tb % 4 + 1) * 128],
                     VWB[eh][:, k * 1024 + eo * 512:
                             k * 1024 + (eo + 1) * 512],
                     start=(k == 0), stop=(k == 15),
                     sem=sMM if k == 15 else None)
            v_mm.append(sMM.n)
            R.wait("vector", sMM, sMM.n)
            sl = idx % 4
            stg = AR_E[:, sl * 512:(sl + 1) * 512]
            st_e[sl].wait("vector")
            R.op("vector", "tensor_copy", stg, pb, sem=sEV)
            v_evac.append(sEV.n)
            R.wait("sync", sEV, sEV.n)
            st_e[sl].dma("sync",
                         v_own[tb * 128:(tb + 1) * 128,
                               eh * 1024 + eo * 512:
                               eh * 1024 + (eo + 1) * 512], stg)
    for sl4 in st_e:
        sl4.wait("gpsimd")
    emit_gather(v_own[1792:2048, :].opt(), v_pair[3584:4096, :].opt(),
                ccv[7], PAIRS)

    barrier()
    if upto < 5:
        return finish(R, nc, es)

    head_mm = []
    head_ev = []
    vp_views = [v_pair[ch * 512:ch * 512 + 256, :]
                .rearrange("(b k) (hh dd) -> hh k b dd", k=128, dd=128)
                for ch in range(8)]
    vo_view = v_own.rearrange("(b k) (hh dd) -> hh k b dd", k=128, dd=128)
    for sl4 in st_e:
        sl4.wait("sync")
    st_wt[0].wait("vector")
    st_wt[1].wait("vector")
    kv_t = {}

    pe3 = [None]

    def emit_head_loads(hh2):
        par2 = hh2 % 2
        if hh2 >= 2:
            R.wait("sync", sMM, head_mm[hh2 - 2])
            R.wait("sync", sEV, head_ev[hh2 - 2])
        if hh2 % 4 == 0:
            R.wait("sync", cck[hh2 // 4], 1)
        kt = sl_K[par2].dma(
            "sync", AR_W[:, par2 * 4096:par2 * 4096 + 2048],
            k_pair[(hh2 // 4) * 1024 + (hh2 % 4) * 128:
                   (hh2 // 4) * 1024 + (hh2 % 4 + 1) * 128, :])
        kt = sl_K[par2].dma("sync",
                            AR_W[:, par2 * 4096 + 2048:(par2 + 1) * 4096],
                            k_own[hh2 * 128:(hh2 + 1) * 128, :])
        if hh2 == 0:
            for ch in range(8):
                R.wait("sync", ccv[ch], 1)
        for vb in range(8):
            base = 8192 + par2 * 4096 + vb * 512
            if vb < 4:
                for sub in range(2):
                    vt = sl_V[par2].dma(
                        "sync",
                        AR_W[:, base + sub * 256:base + (sub + 1) * 256]
                        .rearrange("p (b dd) -> p b dd", dd=128),
                        vp_views[vb * 2 + sub][hh2][:, :, :])
            else:
                vt = sl_V[par2].dma(
                    "sync",
                    AR_W[:, base:base + 512]
                    .rearrange("p (b dd) -> p b dd", dd=128),
                    vo_view[hh2][:, (vb - 4) * 4:(vb - 3) * 4, :])
        gt = sl_G[par2].dma(
            "sync", AR_D[:, 4096 + par2 * 2048: 4096 + (par2 + 1) * 2048],
            g_s[hh2, :, :])
        kv_t[hh2] = (kt, sl_V[par2].sem.n, gt)

    emit_head_loads(0)
    emit_head_loads(1)
    for h in range(NH):
        par = h % 2
        k_t, v_t, g_t = kv_t[h]
        Kh = AR_W[:, par * 4096:(par + 1) * 4096]
        Vh = AR_W[:, 8192 + par * 4096: 8192 + (par + 1) * 4096]
        Gh = AR_D[:, 4096 + par * 2048: 4096 + (par + 1) * 2048]
        R.wait("tensor", sl_K[par].sem, k_t)
        R.wait("tensor", sl_V[par].sem, v_t)
        exp_done = {}
        av_done = {}
        mask_done = {}
        all_b = []
        for qt in range(4):
            nk = 20 + 4 * qt
            bs = []
            for lo in range(0, 16, 3):
                bs.append((lo, min(lo + 3, 16), 0))
            for lo in range(16, nk, 3):
                bs.append((lo, min(lo + 3, nk), 1))
            for bj, (cs, ce, reg) in enumerate(bs):
                all_b.append((qt, nk, cs, ce, reg, bj == 0,
                              bj == len(bs) - 1))

        def emit_av(gi):
            qt, nk, cs, ce, reg, first, last = all_b[gi]
            Ebase = (gi % 2) * 1536
            R.wait("tensor", sEX, exp_done[gi])
            if gi in mask_done:
                R.wait("tensor", sEV, mask_done[gi])
            if first and pe3[0] is not None:
                R.wait("tensor", sEX if False else sEV, pe3[0])
            for ci in range(ce - cs):
                c = cs + ci
                Esl = AR_D[:, Ebase + ci * 512:Ebase + (ci + 1) * 512]
                R.mm("tensor", PS(6), Vh[:, c * 128:(c + 1) * 128], Esl,
                     start=(c == 0), stop=(c == nk - 1))
                av_done[gi] = R.mm(
                    "tensor", PS(7), ONES[:, :], Esl,
                    start=(c == 0), stop=(c == nk - 1),
                    sem=sMM if ci == ce - cs - 1 else None)
            if last:
                last_av = av_done[gi]
                R.wait("scalar", sMM, last_av)
                rsf = AR_F[:, 4096:4608]
                R.op("scalar", "activation", rsf, PS(7), AF.Ln, sem=sEX)
                R.wait("scalar", sEX, sEX.n)
                R.op("scalar", "activation", rsf, rsf, AF.Exp, scale=-1.0,
                     sem=sEX)
                rs_t = sEX.n
                R.wait("vector", sEX, rs_t)
                R.wait("vector", sMM, last_av)
                R.wait("vector", sl_G[par].sem, g_t)
                avn = AR_F[:, 4608:5120]
                R.op("vector", "tensor_tensor", avn, PS(6), rsf,
                     op=ALU.mult, sem=sEV)
                R.wait("vector", sEV, sEV.n)
                R.op("vector", "tensor_tensor",
                     AR_GAV[:, h * 2048 + qt * 512:
                            h * 2048 + (qt + 1) * 512],
                     avn, Gh[:, qt * 512:(qt + 1) * 512], op=ALU.mult,
                     sem=sEV)
                pe3[0] = sEV.n

        for gi, (qt, nk, cs, ce, reg, first, last) in enumerate(all_b):
            blen = ce - cs
            sh = (gi % 2) * 3
            Ebase = (gi % 2) * 1536
            if gi >= 2:
                R.wait("tensor", sEX, exp_done[gi - 2])
            for ci in range(blen):
                R.mm("tensor", PS(sh + ci),
                     Kh[:, (cs + ci) * 128:(cs + ci + 1) * 128],
                     AR_ACT[:, h * 2048 + qt * 512:
                            h * 2048 + (qt + 1) * 512],
                     start=True, stop=True,
                     sem=sMM if ci == blen - 1 else None)
            qk_t = sMM.n
            R.wait("scalar", sMM, qk_t)
            if gi >= 2:
                R.wait("scalar", sMM, av_done[gi - 2])
            R.op("scalar", "activation",
                 AR_D[:, Ebase:Ebase + blen * 512], PS(sh, blen),
                 AF.Exp, scale=SCALE_QK, bias=KB[:, reg:reg + 1], sem=sEX)
            exp_done[gi] = sEX.n
            need_mask = [ci for ci in range(blen) if cs + ci >= nk - 4]
            if need_mask:
                R.wait("vector", sEX, exp_done[gi])
                for ci in need_mask:
                    j = cs + ci - (nk - 4)
                    sl = AR_D[:, Ebase + ci * 512:Ebase + (ci + 1) * 512]
                    R.op("vector", "tensor_tensor", sl, sl,
                         MASKS[:, j * 512:(j + 1) * 512],
                         op=ALU.mult, sem=sEV)
                mask_done[gi] = sEV.n
            if gi >= 1:
                emit_av(gi - 1)
        emit_av(len(all_b) - 1)
        head_mm.append(sMM.n)
        head_ev.append(sEV.n)
        if h == 2:
            st_wt[0].wait("gpsimd")
            st_wt[1].wait("gpsimd")
            R.wait("gpsimd", sMM, head_mm[1])
            emit_gather(wqgT_loc.opt(), wqgT_all.opt(), ccg, ALL8)
        if h == 7:
            R.wait("gpsimd", sMM, head_mm[6])
            emit_gather(wqpT_loc.opt(), wqpT_all.opt(), ccp, ALL8)
        if h + 2 < NH:
            emit_head_loads(h + 2)
    barrier()
    if upto < 6:
        return finish(R, nc, es)

    o_evac = []
    o_mm = []
    p4_at = [AR_D[:, 2048:4096], AR_D[:, 4096:6144], AR_D[:, 6144:8192]]
    p4_xt = [AR_F[:, 0:2048], AR_F[:, 2048:4096]]
    p4_sq = AR_D[:, 0:2048]
    x1ev_hist = {}
    x1st_hist = {}
    ev2_hist = {}
    ln2_tmm = {}
    ln2c_ev = {}

    def ln_stats2(src, c0, sem_eng_eps):
        s1 = STAT[:, c0:c0 + 1]
        s2 = STAT[:, c0 + 1:c0 + 2]
        negm = STAT[:, c0 + 2:c0 + 3]
        m2t = STAT[:, c0 + 3:c0 + 4]
        varr = STAT[:, c0 + 4:c0 + 5]
        rstd = STAT[:, c0 + 5:c0 + 6]
        R.op("vector", "scalar_tensor_tensor", p4_sq, src, 1.0, src,
             op0=ALU.mult, op1=ALU.mult, accum_out=s2, sem=sEV)
        R.op("vector", "reduce_sum", s1, src, axis=AX.X, sem=sEV)
        R.wait("vector", sEV, sEV.n)
        R.op("vector", "tensor_scalar", negm, s1, -1.0 / D, None,
             op0=ALU.mult, sem=sEV)
        R.wait("vector", sEV, sEV.n)
        R.op("vector", "tensor_scalar", m2t, negm, negm, None,
             op0=ALU.mult, sem=sEV)
        R.wait("vector", sEV, sEV.n)
        R.op("vector", "scalar_tensor_tensor", varr, s2, 1.0 / D, m2t,
             op0=ALU.mult, op1=ALU.subtract, sem=sEV)
        R.wait("scalar", sEV, sEV.n)
        R.op("scalar", "activation", rstd, varr, AF.Ln, bias=EPSL[:],
             sem=sEX)
        R.wait("scalar", sEX, sEX.n)
        R.op("scalar", "activation", rstd, rstd, AF.Exp, scale=-0.5, sem=sEX)
        return sEX.n

    LN2V = AR_ACT.rearrange("p (k c) -> p k c", c=2048)

    def emit_ln2T(qb):
        par = qb % 2
        PT = psum[:, 2048 + par * 1024:
                  2048 + (par + 1) * 1024].bitcast(BF16)
        if qb >= 2:
            R.wait("tensor", sEX, ln2c_ev[qb - 2])
        R.wait("tensor", sEV, ev2_hist[qb])
        src_at = p4_at[qb % 3]
        for k in range(16):
            R.op("tensor", "transpose", PT[:, k * 128:(k + 1) * 128],
                 src_at[:, k * 128:(k + 1) * 128], IDN[:],
                 sem=sMM if k == 15 else None)
        ln2_tmm[qb] = sMM.n
        R.wait("scalar", sMM, ln2_tmm[qb])
        R.op("scalar", "activation",
             LN2V[:, :, qb * 128:(qb + 1) * 128],
             PT.rearrange("p (k c) -> p k c", c=128), AF.Copy,
             sem=sEX)
        ln2c_ev[qb] = sEX.n

    def p4b_tile(qb):
        par = qb % 2
        at = p4_at[qb % 3]
        xt = p4_xt[par]
        c0 = 16 + par * 16
        c1 = c0 + 8
        if qb >= 2:
            emit_ln2T(qb - 2)
        if qb >= 3:
            R.wait("scalar", sMM, ln2_tmm[qb - 3])
        for sl4 in st_e:
            sl4.wait("scalar")
        a_t = sl_xa.dma("scalar", at, attn_s[qb * 128:(qb + 1) * 128, :])
        if qb >= 2:
            R.wait("scalar", sEV, ev2_hist[qb - 2])
            R.wait("scalar", st_x1[par].sem, x1st_hist[qb - 2])
        x_t = sl_xb.dma("scalar", xt,
                        x_in[2048 + qb * 128: 2048 + (qb + 1) * 128, :])
        R.wait("vector", sl_xa.sem, a_t)
        rt1 = ln_stats2(at, c0, None)
        rstd1 = STAT[:, c0 + 5:c0 + 6]
        negm1 = STAT[:, c0 + 2:c0 + 3]
        nmr = STAT[:, c0 + 6:c0 + 7]
        R.wait("vector", sEX, rt1)
        R.op("vector", "tensor_scalar", nmr, negm1, rstd1, None,
             op0=ALU.mult, sem=sEV)
        R.wait("vector", sEV, sEV.n)
        R.wait("vector", sl_xb.sem, x_t)
        R.op("vector", "tensor_scalar", xt, xt, nmr, None,
             op0=ALU.add, sem=sEV)
        R.wait("vector", sEV, sEV.n)
        R.op("vector", "scalar_tensor_tensor", xt, at, rstd1, xt,
             op0=ALU.mult, op1=ALU.add, sem=sEV)
        x1ev_hist[qb] = sEV.n
        R.wait("gpsimd", sEV, x1ev_hist[qb])
        x1st_hist[qb] = st_x1[par].dma("gpsimd",
                                       x1_s[qb * 128:(qb + 1) * 128, :], xt)
        R.wait("vector", sEV, x1ev_hist[qb])
        rt2 = ln_stats2(xt, c1, None)
        R.wait("vector", sEX, rt2)
        R.op("vector", "tensor_scalar", at, xt,
             STAT[:, c1 + 2:c1 + 3], STAT[:, c1 + 5:c1 + 6],
             op0=ALU.add, op1=ALU.mult, sem=sEV)
        ev2_hist[qb] = sEV.n

    for eoh in range(2):
        if eoh == 1:
            R.wait("sync", sMM, o_mm[-1])
        ot_last = 0
        for e in range(16):
            ot_last = sl_wh.dma(
                "sync", AR_W[:, e * 1024:(e + 1) * 1024],
                wTo_s[e * 128:(e + 1) * 128,
                      eoh * 1024:(eoh + 1) * 1024])
        R.wait("tensor", sl_wh.sem, ot_last)
        for qb in range(16):
            for eot in range(2):
                idx = len(o_evac)
                if idx >= 4:
                    R.wait("tensor", sEV, o_evac[idx - 4])
                pb = PS(idx % 4)
                for e in range(16):
                    R.mm("tensor", pb,
                         AR_GAV[:, e * 2048 + qb * 128:
                                e * 2048 + (qb + 1) * 128],
                         AR_W[:, e * 1024 + eot * 512:
                              e * 1024 + (eot + 1) * 512],
                         start=(e == 0), stop=(e == 15),
                         sem=sMM if e == 15 else None)
                o_mm.append(sMM.n)
                R.wait("vector", sMM, o_mm[-1])
                sl = idx % 4
                stg = AR_E[:, sl * 512:(sl + 1) * 512]
                st_e[sl].wait("vector")
                R.op("vector", "tensor_copy", stg, pb, sem=sEV)
                o_evac.append(sEV.n)
                R.wait("sync", sEV, o_evac[-1])
                st_e[sl].dma("sync",
                             attn_s[qb * 128:(qb + 1) * 128,
                                    eoh * 1024 + eot * 512:
                                    eoh * 1024 + (eot + 1) * 512], stg)
            if eoh == 1:
                p4b_tile(qb)
    emit_ln2T(14)
    emit_ln2T(15)
    if upto < 8:
        barrier()
        return finish(R, nc, es)

    hs_pool = []
    h_evac = []
    gu_mm = []
    gu_mm_pr = {}
    wq2_t = {}
    R.wait("sync", sMM, o_mm[-1])
    R.wait("sync", ccg, 1)
    R.wait("scalar", sEX, ln2c_ev[15])
    R.wait("gpsimd", sMM, ln2_tmm[15])

    def load_gu(pr):
        par = pr % 2
        rg, ru = pr // 4, 4 + pr // 4
        hl0 = (pr % 4) * 512
        if pr >= 2:
            R.wait("sync", sMM, gu_mm_pr[pr - 2])
        for kh in range(2):
            sl_wq2.dma(
                "sync",
                AR_GAV[:, par * 16384 + kh * 4096:
                       par * 16384 + (kh + 1) * 4096]
                .rearrange("p (k f) -> p k f", f=512),
                wqgT_all[rg * 2048 + kh * 1024: rg * 2048 + (kh + 1) * 1024,
                         hl0:hl0 + 512]
                .rearrange("(k p) f -> p k f", p=128))
        for kh in range(2):
            wq2_t[pr] = sl_wq2.dma(
                "sync",
                AR_GAV[:, par * 16384 + 8192 + kh * 4096:
                       par * 16384 + 8192 + (kh + 1) * 4096]
                .rearrange("p (k f) -> p k f", f=512),
                wqgT_all[ru * 2048 + kh * 1024: ru * 2048 + (kh + 1) * 1024,
                         hl0:hl0 + 512]
                .rearrange("(k p) f -> p k f", p=128))

    def emit_ones(j):
        hb63 = (j // 4 == 63)
        h2j = AR_D[:, 2048 + (j % 2) * 512: 2048 + (j % 2) * 512 + 512]
        R.wait("tensor", sPO, hs_pool[j])
        R.mm("tensor", PS(4 + j % 4), ONES[:, :], h2j,
             start=(j // 4 == 0), stop=hb63, sem=sMM if hb63 else None)

    load_gu(0)
    load_gu(1)
    R.wait("tensor", sEV, o_evac[-1])
    for pr in range(16):
        par = pr % 2
        R.wait("tensor", sl_wq2.sem, wq2_t[pr])
        for hb in range(4):
            hblk = pr * 4 + hb
            for tt in range(4):
                it = hblk * 4 + tt
                ba = (it % 2) * 2
                if pr == 0 and hb == 0:
                    R.wait("tensor", sEX, ln2c_ev[(tt + 1) * 4 - 1])
                if it >= 2:
                    R.wait("tensor", sEV, h_evac[it - 2])
                for k in range(16):
                    R.mm("tensor", PS(ba),
                         AR_GAV[:, par * 16384 + k * 512 + hb * 128:
                                par * 16384 + k * 512 + (hb + 1) * 128],
                         AR_ACT[:, k * 2048 + tt * 512:
                                k * 2048 + (tt + 1) * 512],
                         start=(k == 0), stop=(k == 15))
                for k in range(16):
                    R.mm("tensor", PS(ba + 1),
                         AR_GAV[:, par * 16384 + 8192 + k * 512 + hb * 128:
                                par * 16384 + 8192 + k * 512 + (hb + 1) * 128],
                         AR_ACT[:, k * 2048 + tt * 512:
                                k * 2048 + (tt + 1) * 512],
                         start=(k == 0), stop=(k == 15),
                         sem=sMM if k == 15 else None)
                gu_mm.append(sMM.n)
                if it >= 1:
                    emit_ones(it - 1)
                R.wait("scalar", sMM, gu_mm[-1])
                sg = AR_D[:, (it % 2) * 1024:(it % 2) * 1024 + 512]
                ht = AR_D[:, (it % 2) * 1024 + 512:(it % 2) * 1024 + 1024]
                h2 = AR_D[:, 2048 + (it % 2) * 512:
                          2048 + (it % 2) * 512 + 512]
                if it >= 2:
                    R.wait("scalar", sEV, h_evac[it - 2])
                R.op("scalar", "activation", sg, PS(ba), AF.Silu, sem=sEX)
                R.wait("vector", sEX, sEX.n)
                if it >= 2:
                    R.wait("vector", sPO, hs_pool[it - 2])
                    st_h[it % 2].wait("vector")
                R.op("vector", "tensor_tensor", ht, sg, PS(ba + 1),
                     op=ALU.mult, sem=sEV)
                h_evac.append(sEV.n)
                R.wait("gpsimd", sEV, sEV.n)
                R.op("gpsimd", "tensor_tensor", h2, ht, ht, op=ALU.mult,
                     sem=sPO)
                hs_pool.append(sPO.n)
                R.wait("sync", sEV, h_evac[-1])
                st_h[it % 2].dma("sync",
                                 h_s[hblk * 128:(hblk + 1) * 128,
                                     tt * 512:(tt + 1) * 512], ht)
        gu_mm_pr[pr] = sMM.n
        if pr + 2 < 16:
            load_gu(pr + 2)
    emit_ones(255)
    rms_mm = sMM.n
    barrier()
    if upto < 9:
        return finish(R, nc, es)

    R.wait("scalar", sMM, rms_mm)
    for tt in range(4):
        R.op("scalar", "activation", RRMS[:, tt * 512:(tt + 1) * 512],
             PS(4 + tt), AF.Ln, scale=1.0 / HH, bias=EPSR[:], sem=sEX)
        R.wait("scalar", sEX, sEX.n)
        R.op("scalar", "activation", RRMS[:, tt * 512:(tt + 1) * 512],
             RRMS[:, tt * 512:(tt + 1) * 512], AF.Exp, scale=-0.5, sem=sEX)
    rrms_t = sEX.n
    R.wait("vector", sEX, rrms_t)
    for sh2 in st_h:
        sh2.wait("sync")

    f_evac = []
    p_mm = []
    mid_mm = {}
    end_mm = {}
    hx_t = {}
    wp_t = {}

    def load_wp(g):
        r = g % 8
        for half in range(2):
            if g >= 1:
                R.wait("scalar", sMM,
                       mid_mm[g - 1] if half == 0 else end_mm[g - 1])
            for qq in range(2):
                o0 = half * 8192 + qq * 4096
                s0 = r * HH + half * 4096 + qq * 2048
                wp_t[(g, half)] = sl_wp[half].dma(
                    "scalar",
                    AR_W[:, o0:o0 + 4096]
                    .rearrange("p (hc e) -> p hc e", e=256),
                    wqpT_all[s0:s0 + 2048, 0:256]
                    .rearrange("(hc p) e -> p hc e", p=128))

    R.wait("tensor", sEX, rrms_t)
    R.wait("scalar", ccp, 1)
    load_wp(0)
    for ttx in range(2):
        if ttx == 1:
            R.wait("sync", sMM, end_mm[7])
        for hc in range(64):
            dstA = (AR_ACT[:, hc * 1024:(hc + 1) * 1024] if hc < 32 else
                    AR_GAV[:, (hc - 32) * 1024:(hc - 31) * 1024])
            hx_t[(ttx, hc)] = sl_hx.dma(
                "sync", dstA,
                h_s[hc * 128:(hc + 1) * 128, ttx * 1024:(ttx + 1) * 1024])
        for eo in range(4):
            for ebg in range(2):
                gidx = (ttx * 8 + eo * 2 + ebg)
                bb = (gidx % 2) * 4
                e0 = eo * 512 + ebg * 256
                if gidx >= 2:
                    R.wait("tensor", sEV, f_evac[4 * (gidx - 2) + 3])
                for hc in range(64):
                    half = hc // 32
                    if hc == 0:
                        R.wait("tensor", sl_wp[0].sem, wp_t[(gidx, 0)])
                    if hc == 32:
                        R.wait("tensor", sl_wp[1].sem, wp_t[(gidx, 1)])
                    R.wait("tensor", sl_hx.sem, hx_t[(ttx, hc)])
                    hsrc = (AR_ACT[:, hc * 1024:(hc + 1) * 1024] if hc < 32
                            else AR_GAV[:, (hc - 32) * 1024:(hc - 31) * 1024])
                    w0 = half * 8192 + (hc % 32) * 256
                    for eb_in in range(2):
                        for tt_in in range(2):
                            R.mm("tensor", PS(bb + eb_in * 2 + tt_in),
                                 AR_W[:, w0 + eb_in * 128:
                                      w0 + (eb_in + 1) * 128],
                                 hsrc[:, tt_in * 512:(tt_in + 1) * 512],
                                 start=(hc == 0), stop=(hc == 63),
                                 sem=sMM if (eb_in == 1 and tt_in == 1)
                                 else None)
                    if hc == 31:
                        mid_mm[gidx] = sMM.n
                end_mm[gidx] = sMM.n
                p_mm.append(sMM.n)
                if gidx + 1 < 16:
                    load_wp(gidx + 1)
                R.wait("vector", sMM, end_mm[gidx])
                for eb_in in range(2):
                    for tt_in in range(2):
                        sidx = len(f_evac)
                        sl = sidx % 4
                        stg = AR_D[:, sl * 512:(sl + 1) * 512]
                        st_f[sl].wait("vector")
                        R.op("vector", "tensor_tensor", stg,
                             PS(bb + eb_in * 2 + tt_in),
                             RRMS[:, (ttx * 2 + tt_in) * 512:
                                  (ttx * 2 + tt_in + 1) * 512],
                             op=ALU.mult, sem=sEV)
                        f_evac.append(sEV.n)
                        R.wait("sync", sEV, sEV.n)
                        r0 = e0 + eb_in * 128
                        st_f[sl].dma("sync",
                                     f_s[r0:r0 + 128,
                                         ttx * 1024 + tt_in * 512:
                                         ttx * 1024 + (tt_in + 1) * 512], stg)
    barrier()
    if upto < 10:
        return finish(R, nc, es)

    for sf in st_f:
        sf.wait("sync")
    st_x1[0].wait("sync")
    st_x1[1].wait("sync")
    p7_ev = {}
    p7_st = {}
    for qb in range(16):
        par = qb % 2
        ft = p4_at[par]
        xt = p4_xt[par]
        c0 = 16 + par * 16
        if qb >= 2:
            R.wait("sync", sEV, p7_ev[qb - 2])
        f_t = sl_ft.dma("sync", ft, f_s[:, qb * 128:(qb + 1) * 128],
                        transpose=True)
        if qb >= 2:
            R.wait("scalar", st_out[par].sem, p7_st[qb - 2])
        x_t = sl_xb.dma("scalar", xt, x1_s[qb * 128:(qb + 1) * 128, :])
        R.wait("vector", sl_ft.sem, f_t)
        rt = ln_stats2(ft, c0, None)
        rstd = STAT[:, c0 + 5:c0 + 6]
        negm = STAT[:, c0 + 2:c0 + 3]
        nmr = STAT[:, c0 + 6:c0 + 7]
        R.wait("vector", sEX, rt)
        R.op("vector", "tensor_scalar", nmr, negm, rstd, None,
             op0=ALU.mult, sem=sEV)
        R.wait("vector", sEV, sEV.n)
        R.wait("vector", sl_xb.sem, x_t)
        R.op("vector", "tensor_scalar", xt, xt, nmr, None,
             op0=ALU.add, sem=sEV)
        R.wait("vector", sEV, sEV.n)
        R.op("vector", "scalar_tensor_tensor", xt, ft, rstd, xt,
             op0=ALU.mult, op1=ALU.add, sem=sEV)
        p7_ev[qb] = sEV.n
        R.wait("gpsimd", sEV, p7_ev[qb])
        p7_st[qb] = st_out[par].dma("gpsimd",
                               out[qb * 128:(qb + 1) * 128, :], xt)
    st_out[0].wait("gpsimd")
    st_out[1].wait("gpsimd")
    barrier()
    st_out[0].wait("sync")
    st_out[1].wait("sync")
    return finish(R, nc, es)


def make_inputs(core, full):
    seq, half = core // 2, core % 2
    x = np.asarray(full["L0_attn_x"], dtype=np.float32)
    ctx = x[seq, 0:2048]
    own = x[seq, half * 2048:(half + 1) * 2048]
    x_core = np.ascontiguousarray(np.concatenate([ctx, own], axis=0))
    kb = np.empty((128, 2), np.float32)
    kb[:, 0] = (MSHIFT if half == 1 else -100000.0)
    kb[:, 1] = MSHIFT
    return {
        "x_in": x_core,
        "kb_in": kb,
        "gb_in": np.ascontiguousarray(np.asarray(full["L0_attn_gate_b"],
                                                 np.float32)),
        "q_w": np.ascontiguousarray(np.asarray(full["L0_attn_q_w"], np.float32)),
        "k_w": np.ascontiguousarray(np.asarray(full["L0_attn_k_w"], np.float32)),
        "v_w": np.ascontiguousarray(np.asarray(full["L0_attn_v_w"], np.float32)),
        "o_w": np.ascontiguousarray(np.asarray(full["L0_attn_o_w"], np.float32)),
        "gate_w": np.ascontiguousarray(np.asarray(full["L0_attn_gate_w"],
                                                  np.float32)),
        "w_gu": np.ascontiguousarray(np.asarray(full["L0_ffn_w_gate_up"],
                                                np.float32)
                                     [core * 2048:(core + 1) * 2048]),
        "w_pr": np.ascontiguousarray(np.asarray(full["L0_ffn_w_proj"],
                                                np.float32)
                                     [core * 256:(core + 1) * 256]),
    }


def kernel(**inputs):
    nc = build_program()
    in_maps = [make_inputs(c, inputs) for c in range(8)]
    res = run_bass_kernel_spmd(nc, in_maps, core_ids=list(range(8)))
    outp = np.empty((4, 4096, 2048), np.float32)
    for c in range(8):
        seq, half = c // 2, c % 2
        outp[seq, half * 2048:(half + 1) * 2048] = res.results[c]["out"]
    return outp


# version 30
# speedup vs baseline: 1.0044x; 1.0044x over previous
import numpy as np
import ml_dtypes
import concourse.bass as bass
import concourse.mybir as mybir
from concourse.bass_utils import run_bass_kernel_spmd
from contextlib import ExitStack

F32 = mybir.dt.float32
BF16 = mybir.dt.bfloat16
bf16 = ml_dtypes.bfloat16
AF = mybir.ActivationFunctionType
ALU = mybir.AluOpType
AX = mybir.AxisListType

ENGINES = ("tensor", "vector", "scalar", "gpsimd", "sync")

D = 2048
SEQ = 4096
TOWN = 2048
NH = 16
HF = 16384
HH = 8192
MSHIFT = -16.0
SCALE_QK = float(np.float32(1.0) / np.sqrt(np.float32(128.0)))
EPS_LN = 1e-5
EPS_RMS = float(np.finfo(np.float32).eps)


class S:
    def __init__(self, nc, es, name):
        self.h = es.enter_context(nc.semaphore(name))
        self.n = 0

    def bump(self, k):
        self.n += k
        return self.n


class Rec:
    def __init__(self):
        self.lists = {e: [] for e in ENGINES}

    def add(self, eng, fn):
        self.lists[eng].append(fn)

    def wait(self, eng, sem, val):
        if val <= 0:
            return
        self.add(eng, lambda q, s=sem.h, v=val: q.wait_ge(s, v))

    def op(self, eng, name, *args, sem=None, inc=1, **kw):
        tgt = sem.bump(inc) if sem is not None else None

        def f(q, nm=name, aa=args, kk=dict(kw), s=sem, ii=inc):
            inst = getattr(q, nm)(*aa, **kk)
            if s is not None:
                inst.then_inc(s.h, ii)

        self.add(eng, f)
        return tgt

    def mm(self, eng, out, lhsT, rhs, start, stop, sem=None, inc=1):
        tgt = sem.bump(inc) if sem is not None else None

        def f(q, o=out, a=lhsT, b=rhs, st=start, sp=stop, s=sem, ii=inc):
            inst = q.matmul(o, a, b, start=st, stop=sp)
            if s is not None:
                inst.then_inc(s.h, ii)

        self.add(eng, f)
        return tgt


class Slot:

    def __init__(self, R, nc, es, name):
        self.R = R
        self.sem = S(nc, es, name)

    def dma(self, eng, out, in_, **kw):
        tgt = self.sem.bump(16)

        def f(q, o=out, i=in_, s=self.sem.h, k=dict(kw)):
            q.dma_start(out=o, in_=i, **k).then_inc(s, 16)

        self.R.add(eng, f)
        return tgt

    def wait(self, eng, tgt=None):
        self.R.wait(eng, self.sem, self.sem.n if tgt is None else tgt)


def finish(R, nc, es):
    with nc.Block() as block:
        @block.tensor
        def _(q):
            for f in R.lists["tensor"]:
                f(q)

        @block.vector
        def _(q):
            for f in R.lists["vector"]:
                f(q)

        @block.scalar
        def _(q):
            for f in R.lists["scalar"]:
                f(q)

        @block.gpsimd
        def _(q):
            for f in R.lists["gpsimd"]:
                f(q)

        @block.sync
        def _(q):
            for f in R.lists["sync"]:
                f(q)

    es.close()
    return nc


def build_program(dbg=(), upto=99):
    dbg = set(dbg)
    nc = bass.Bass("TRN2", target_bir_lowering=False, debug=False)

    def dram(name, shape, dtype, kind="Internal"):
        k = "ExternalOutput" if (name in dbg and kind == "Internal") else kind
        return nc.dram_tensor(name, shape, dtype, kind=k).ap()

    x_in = dram("x_in", [SEQ, D], F32, "ExternalInput")
    kb_in = dram("kb_in", [128, 2], F32, "ExternalInput")
    gb_in = dram("gb_in", [128], F32, "ExternalInput")
    q_w = dram("q_w", [D, D], F32, "ExternalInput")
    k_w = dram("k_w", [D, D], F32, "ExternalInput")
    v_w = dram("v_w", [D, D], F32, "ExternalInput")
    o_w = dram("o_w", [D, D], F32, "ExternalInput")
    gate_w = dram("gate_w", [128, 128], F32, "ExternalInput")
    w_gu = dram("w_gu", [2048, D], F32, "ExternalInput")
    w_pr = dram("w_pr", [256, HH], F32, "ExternalInput")
    out = dram("out", [TOWN, D], F32, "ExternalOutput")

    ln1T_s = dram("ln1T_s", [D, TOWN], BF16)
    wTq_s = dram("wTq_s", [D, D], BF16)
    wTk_s = dram("wTk_s", [D, D], BF16)
    wTv_s = dram("wTv_s", [D, D], BF16)
    wTo_s = dram("wTo_s", [D, D], BF16)
    wTg_s = dram("wTg_s", [128, 128], BF16)
    k_own = dram("k_own", [NH * 128, TOWN], BF16)
    k_pair = dram("k_pair", [2 * NH * 128, TOWN], BF16)
    v_own = dram("v_own", [TOWN, D], BF16)
    v_pair = dram("v_pair", [2 * TOWN, D], BF16)
    g_s = dram("g_s", [NH, 128, TOWN], BF16)
    attn_s = dram("attn_s", [TOWN, D], BF16)
    x1_s = dram("x1_s", [TOWN, D], F32)
    wqgT_loc = dram("wqgT_loc", [D, 2048], BF16)
    wqpT_loc = dram("wqpT_loc", [HH, 256], BF16)
    wqgT_all = dram("wqgT_all", [8 * D, 2048], BF16)
    wqpT_all = dram("wqpT_all", [8 * HH, 256], BF16)
    h_s = dram("h_s", [HH, TOWN], BF16)
    f_s = dram("f_s", [D, TOWN], BF16)

    es = ExitStack()

    def sbuf(name, shape, dtype):
        return es.enter_context(nc.sbuf_tensor(name, shape, dtype))

    AR_W = sbuf("AR_W", [128, 16384], BF16)
    AR_ACT = sbuf("AR_ACT", [128, 32768], BF16)
    AR_GAV = sbuf("AR_GAV", [128, 32768], BF16)
    AR_D = sbuf("AR_D", [128, 8192], BF16)
    AR_E = sbuf("AR_E", [128, 2048], BF16)
    AR_F = sbuf("AR_F", [128, 5120], F32)
    STAT = sbuf("STAT", [128, 64], F32)
    QST = sbuf("QST", [128, 128], F32)
    KB = sbuf("KB", [128, 2], F32)
    GB = sbuf("GB", [128, 1], F32)
    EPSL = sbuf("EPSL", [128, 1], F32)
    EPSR = sbuf("EPSR", [128, 1], F32)
    ONES = sbuf("ONES", [128, 128], BF16)
    GWT = sbuf("GWT", [128, 128], BF16)
    IDN = sbuf("IDN", [128, 128], BF16)
    MASKS = sbuf("MASKS", [128, 4 * 512], BF16)
    SCB = sbuf("SCB", [128, 32], BF16)

    RRMS = AR_F[:, 0:2048]
    psum = es.enter_context(nc.psum_tensor("psAll", [128, 4096], F32))

    def PS(b, n=1):
        return psum[:, b * 512:(b + n) * 512]

    R = Rec()
    sems = {k: S(nc, es, k) for k in ("bar", "sMM", "sEV", "sEX", "sPO")}
    bar, sMM, sEV, sEX, sPO = (sems[k] for k in
                               ("bar", "sMM", "sEV", "sEX", "sPO"))
    ccg = S(nc, es, "ccg")
    ccp = S(nc, es, "ccp")
    cck = [S(nc, es, f"cck{i}") for i in range(4)]
    ccv = [S(nc, es, f"ccv{i}") for i in range(8)]

    def SL(name):
        return Slot(R, nc, es, name)

    sl_kb = SL("l_kb")
    sl_xa = SL("l_xa")
    sl_xb = SL("l_xb")
    sl_wc = [SL("l_wc0"), SL("l_wc1")]
    sl_tt = [SL("l_tt0"), SL("l_tt1")]
    sl_wh = SL("l_wh")
    sl_K = [SL("l_K0"), SL("l_K1")]
    sl_V = [SL("l_V0"), SL("l_V1")]
    sl_G = [SL("l_G0"), SL("l_G1")]
    sl_hx = SL("l_hx")
    sl_wp = [SL("l_wp0"), SL("l_wp1")]
    sl_qs = [SL("l_qs0"), SL("l_qs1")]
    sl_ft = SL("l_ft")
    sl_wq2 = SL("l_wq2")
    sl_wt = [SL("l_wt0"), SL("l_wt1")]
    st_wt = [SL("s_wt0"), SL("s_wt1")]
    st_ln1 = [SL("s_ln10"), SL("s_ln11")]
    st_wc = [SL("s_wc0"), SL("s_wc1")]
    st_e = [SL("s_e0"), SL("s_e1"), SL("s_e2"), SL("s_e3")]
    st_wq = SL("s_wq")
    st_x1 = [SL("s_x10"), SL("s_x11")]
    st_ln2 = SL("s_ln2")
    st_h = [SL("s_h0"), SL("s_h1")]
    st_f = [SL("s_f0"), SL("s_f1"), SL("s_f2"), SL("s_f3")]
    st_out = [SL("s_out0"), SL("s_out1")]

    def barrier():
        for e in ENGINES:
            R.add(e, lambda q, s=bar.h: q.nop().then_inc(s, 1))
        bar.n += 5
        for e in ENGINES:
            R.wait(e, bar, bar.n)

    def ln_stats(src, sb, sq_dst):
        s1 = STAT[:, sb + 0:sb + 1]
        s2 = STAT[:, sb + 1:sb + 2]
        negm = STAT[:, sb + 2:sb + 3]
        m2t = STAT[:, sb + 3:sb + 4]
        varr = STAT[:, sb + 4:sb + 5]
        rstd = STAT[:, sb + 5:sb + 6]
        R.op("vector", "scalar_tensor_tensor", sq_dst, src, 1.0, src,
             op0=ALU.mult, op1=ALU.mult, accum_out=s2, sem=sEV)
        R.op("vector", "reduce_sum", s1, src, axis=AX.X, sem=sEV)
        R.wait("vector", sEV, sEV.n)
        R.op("vector", "tensor_scalar", negm, s1, -1.0 / D, None,
             op0=ALU.mult, sem=sEV)
        R.wait("vector", sEV, sEV.n)
        R.op("vector", "tensor_scalar", m2t, negm, negm, None,
             op0=ALU.mult, sem=sEV)
        R.wait("vector", sEV, sEV.n)
        R.op("vector", "scalar_tensor_tensor", varr, s2, 1.0 / D, m2t,
             op0=ALU.mult, op1=ALU.subtract, sem=sEV)
        R.wait("scalar", sEV, sEV.n)
        R.op("scalar", "activation", rstd, varr, AF.Ln, bias=EPSL[:],
             sem=sEX)
        R.wait("scalar", sEX, sEX.n)
        R.op("scalar", "activation", rstd, rstd, AF.Exp, scale=-0.5, sem=sEX)
        return sEX.n

    def ln_fin(src, dst, sb, rstd_t):
        R.wait("vector", sEX, rstd_t)
        R.op("vector", "tensor_scalar", dst, src,
             STAT[:, sb + 2:sb + 3], STAT[:, sb + 5:sb + 6],
             op0=ALU.add, op1=ALU.mult, sem=sEV)
        return sEV.n

    q_ev = {}
    q_st = {}
    q_tmm = {}

    def quant_block(j, src, dst):
        G = 32
        A = AR_GAV[:, 8192 + (j % 2) * 2048: 8192 + (j % 2) * 2048 + 2048]
        B = AR_GAV[:, (j % 2) * 4096: (j % 2) * 4096 + 2048]
        C = AR_GAV[:, (j % 2) * 4096 + 2048: (j % 2) * 4096 + 4096]
        gsumt = QST[:, 0:G]
        sct = QST[:, G:2 * G]
        hst = QST[:, 2 * G:3 * G]
        nhst = QST[:, 3 * G:4 * G]
        scb = SCB[:, 0:G]
        if j >= 2:
            R.wait("gpsimd", sEV, q_ev[j - 2])
        ca_t = sl_qs[j % 2].dma("gpsimd", A, src)
        R.wait("vector", sl_qs[j % 2].sem, ca_t)
        R.op("vector", "tensor_reduce", gsumt,
             A.rearrange("p (g k) -> p g k", k=64), axis=AX.X, op=ALU.add,
             apply_absolute_value=True, sem=sEV)
        R.wait("vector", sEV, sEV.n)
        R.op("vector", "tensor_scalar", scb, gsumt, 1.0 / 64, 1e-8,
             op0=ALU.mult, op1=ALU.max, sem=sEV)
        R.wait("vector", sEV, sEV.n)
        R.op("vector", "tensor_copy", sct, scb, sem=sEV)
        R.wait("vector", sEV, sEV.n)
        R.op("vector", "tensor_scalar", hst, sct, 0.5, None, op0=ALU.mult,
             sem=sEV)
        R.op("vector", "tensor_scalar", nhst, sct, -0.5, None, op0=ALU.mult,
             sem=sEV)
        R.wait("vector", sEV, sEV.n)
        if j >= 2:
            R.wait("vector", sMM, q_tmm[j - 2])
        R.op("vector", "tensor_tensor",
             B.rearrange("p (g k) -> p g k", k=64),
             A.rearrange("p (g k) -> p g k", k=64),
             hst[:, :, None].broadcast_to([128, G, 64]), op=ALU.is_gt,
             sem=sEV)
        R.op("vector", "tensor_tensor",
             C.rearrange("p (g k) -> p g k", k=64),
             A.rearrange("p (g k) -> p g k", k=64),
             nhst[:, :, None].broadcast_to([128, G, 64]), op=ALU.is_lt,
             sem=sEV)
        ev_m = sEV.n
        R.wait("gpsimd", sEV, ev_m)
        R.op("gpsimd", "tensor_tensor", A, B, C, op=ALU.subtract, sem=sPO)
        R.wait("vector", sPO, sPO.n)
        R.op("vector", "tensor_tensor",
             B.rearrange("p (g k) -> p g k", k=64),
             A.rearrange("p (g k) -> p g k", k=64),
             scb[:, :, None].broadcast_to([128, G, 64]), op=ALU.mult, sem=sEV)
        q_ev[j] = sEV.n
        R.wait("tensor", sEV, q_ev[j])
        q_tmm[j], q_st[j] = pe_T(B, 2048, dst)
        return q_st[j]

    qjobs = []
    for b in range(16):
        qjobs.append((w_gu[b * 128:(b + 1) * 128, :],
                      wqgT_loc[0:2048, b * 128:(b + 1) * 128]))
    for rb in range(2):
        for cb in range(4):
            qjobs.append((w_pr[rb * 128:(rb + 1) * 128,
                               cb * 2048:(cb + 1) * 2048],
                          wqpT_loc[cb * 2048:(cb + 1) * 2048,
                                   rb * 128:(rb + 1) * 128]))

    sl_kb.dma("sync", KB[:], kb_in[:])
    sl_kb.dma("sync", GB[:], gb_in[:, None])
    R.op("gpsimd", "memset", ONES[:], 1.0, sem=sPO)
    R.op("gpsimd", "memset", IDN[:], 1.0, sem=sPO)
    R.op("gpsimd", "memset", MASKS[:], 1.0, sem=sPO)
    R.wait("gpsimd", sPO, sPO.n)
    for j in range(4):
        R.op("gpsimd", "affine_select", MASKS[:, j * 512:(j + 1) * 512],
             MASKS[:, j * 512:(j + 1) * 512], pattern=[[1, 512]],
             compare_op=ALU.is_ge, fill=0.0, base=-128 * j,
             channel_multiplier=-1, sem=sPO)
    R.op("gpsimd", "affine_select", IDN[:], IDN[:], pattern=[[1, 128]],
         compare_op=ALU.is_ge, fill=0.0, base=0,
         channel_multiplier=-1, sem=sPO)
    R.wait("gpsimd", sPO, sPO.n)
    R.op("gpsimd", "affine_select", IDN[:], IDN[:], pattern=[[-1, 128]],
         compare_op=ALU.is_ge, fill=0.0, base=0,
         channel_multiplier=1, sem=sPO)
    idn_tgt = sPO.n
    R.wait("tensor", sPO, idn_tgt)
    R.op("vector", "memset", EPSL[:], EPS_LN, sem=sEV)
    R.op("vector", "memset", EPSR[:], EPS_RMS, sem=sEV)

    pt_state = {"j": 0, "ev": {}, "st": {}, "mm": {}, "slot": {}}

    def pe_T(src, ncol, dst_col, tsg=None, lag=2):
        j = pt_state["j"]
        pt_state["j"] += 1
        par = j % 2
        nt = ncol // 128
        PT = psum[:, 2048 + par * 1024:
                  2048 + (par + 1) * 1024].bitcast(BF16)
        TSG = (AR_GAV[:, 12288 + par * 2048: 12288 + par * 2048 + 2048]
               if tsg is None else tsg)
        if j >= 2:
            R.wait("tensor", sEX, pt_state["ev"][j - 2])
        for k in range(nt):
            R.op("tensor", "transpose", PT[:, k * 128:(k + 1) * 128],
                 src[:, k * 128:(k + 1) * 128], IDN[:],
                 sem=sMM if k == nt - 1 else None)
        pt_state["mm"][j] = sMM.n
        R.wait("scalar", sMM, pt_state["mm"][j])
        if j >= lag and (j - lag) in pt_state["st"]:
            stp = pt_state["slot"][j - lag]
            R.wait("scalar", stp.sem, pt_state["st"][j - lag])
        R.op("scalar", "activation", TSG[:, 0:ncol], PT[:, 0:ncol], AF.Copy,
             sem=sEX)
        pt_state["ev"][j] = sEX.n
        R.wait("scalar", sEX, pt_state["ev"][j])
        for c0 in range(0, ncol, 1024):
            seg = min(1024, ncol - c0)
            pt_state["st"][j] = st_wt[par].dma(
                "scalar",
                dst_col[c0:c0 + seg, :].rearrange("(k p) c -> p k c", p=128),
                TSG[:, c0:c0 + seg].rearrange("p (k c) -> p k c", c=128))
        pt_state["slot"][j] = st_wt[par]
        return pt_state["mm"][j], pt_state["st"][j]

    wcast = []
    for wsrc, wdstT in ((q_w, wTq_s), (k_w, wTk_s), (v_w, wTv_s),
                        (o_w, wTo_s)):
        for blk in range(16):
            wcast.append((wsrc[blk * 128:(blk + 1) * 128, :],
                          wdstT[0:2048, blk * 128:(blk + 1) * 128], 2048))
    wcast.append((gate_w[:, :], wTg_s[0:128, 0:128], 128))
    wc_n = [0]
    wc_tmm = {}

    def emit_wcast():
        j = wc_n[0]
        if j >= len(wcast):
            return
        wc_n[0] += 1
        src, dstT, ncol = wcast[j]
        par = j % 2
        stg = AR_D[:, par * 2048:par * 2048 + ncol]
        if j >= 2:
            R.wait("gpsimd", sMM, wc_tmm[j - 2])
        t = sl_wc[par].dma("gpsimd", stg, src)
        R.wait("tensor", sl_wc[par].sem, t)
        wc_tmm[j], _ = pe_T(stg, ncol, dstT)

    apply_hist = {}
    store_hist = {}
    xsl = [sl_xa, sl_xb]
    XP = [AR_ACT[:, 0:8192].bitcast(F32), AR_ACT[:, 8192:16384].bitcast(F32)]
    LOUT2 = [AR_ACT[:, 16384:20480], AR_ACT[:, 20480:24576]]
    SQ1 = AR_ACT[:, 24576:26624]

    q_store = {}
    lout_tmm = {}
    qj = [0]
    for p in range(8):
        par = p % 2
        xp = XP[par]
        c = par * 16
        if p >= 2:
            R.wait("sync", sEX, apply_hist[p - 2])
        for t in range(2):
            ld = xsl[par].dma(
                "sync", xp[:, t * 2048:(t + 1) * 2048],
                x_in[(16 + 2 * p + t) * 128:(16 + 2 * p + t + 1) * 128, :])
        for _ in range(9):
            emit_wcast()
        s2p = STAT[:, c:c + 2]
        s1p = STAT[:, c + 2:c + 4]
        ngp = STAT[:, c + 4:c + 6]
        m2p = STAT[:, c + 6:c + 8]
        vrp = STAT[:, c + 8:c + 10]
        rsp = STAT[:, c + 10:c + 12]
        R.wait("vector", xsl[par].sem, ld)
        for t in range(2):
            R.op("vector", "scalar_tensor_tensor",
                 AR_ACT[:, 24576 + t * 2048:26624 + t * 2048],
                 xp[:, t * 2048:(t + 1) * 2048], 1.0,
                 xp[:, t * 2048:(t + 1) * 2048],
                 op0=ALU.mult, op1=ALU.mult,
                 accum_out=s2p[:, t:t + 1], sem=sEV)
            R.op("vector", "reduce_sum", s1p[:, t:t + 1],
                 xp[:, t * 2048:(t + 1) * 2048], axis=AX.X, sem=sEV)
        R.wait("vector", sEV, sEV.n)
        R.op("vector", "tensor_scalar", ngp, s1p, -1.0 / D, None,
             op0=ALU.mult, sem=sEV)
        R.wait("vector", sEV, sEV.n)
        R.op("vector", "tensor_tensor", m2p, ngp, ngp, op=ALU.mult, sem=sEV)
        R.wait("vector", sEV, sEV.n)
        R.op("vector", "scalar_tensor_tensor", vrp, s2p, 1.0 / D, m2p,
             op0=ALU.mult, op1=ALU.subtract, sem=sEV)
        R.wait("scalar", sEV, sEV.n)
        R.op("scalar", "activation", rsp, vrp, AF.Ln, bias=EPSL[:], sem=sEX)
        R.wait("scalar", sEX, sEX.n)
        R.op("scalar", "activation", rsp, rsp, AF.Exp, scale=-0.5, sem=sEX)
        R.wait("vector", sEX, sEX.n)
        nmrp = STAT[:, c + 12:c + 14]
        if p >= 2:
            R.wait("vector", sEX, apply_hist[p - 2])
        R.op("vector", "tensor_tensor", nmrp, ngp, rsp, op=ALU.mult, sem=sEV)
        R.wait("scalar", sEV, sEV.n)
        if p >= 2:
            R.wait("scalar", sMM, lout_tmm[p - 2])
        for t in range(2):
            R.op("scalar", "activation", LOUT2[par][:, t * 2048:
                                                    (t + 1) * 2048],
                 xp[:, t * 2048:(t + 1) * 2048], AF.Identity,
                 scale=STAT[:, c + 10 + t:c + 11 + t],
                 bias=STAT[:, c + 12 + t:c + 13 + t], sem=sEX)
        apply_hist[p] = sEX.n
        R.wait("tensor", sEX, apply_hist[p])
        for t in range(2):
            lout_tmm[p], _ = pe_T(
                LOUT2[par][:, t * 2048:(t + 1) * 2048], 2048,
                ln1T_s[0:2048, (2 * p + t) * 128:(2 * p + t + 1) * 128])
        for _ in range(3):
            q_store[qj[0]] = quant_block(qj[0], *qjobs[qj[0]])
            qj[0] += 1
    while wc_n[0] < len(wcast):
        emit_wcast()
    barrier()
    if upto < 2:
        return finish(R, nc, es)


    def emit_gather(in_ap, out_ap, sem, groups):
        def f(q, s=sem.h, a=in_ap, b=out_ap, g=groups):
            q.collective_compute(
                "AllGather", ALU.bypass,
                replica_groups=g,
                ins=[a], outs=[b]).then_inc(s)

        R.add("gpsimd", f)
        sem.n = 1

    PAIRS = [[0, 1], [2, 3], [4, 5], [6, 7]]
    ALL8 = [[0, 1, 2, 3, 4, 5, 6, 7]]

    TTBUF = [AR_D[:, 0:8192], AR_GAV[:, 0:8192]]
    WHB = [AR_W, AR_GAV[:, 16384:32768]]

    def proj_tt_read(j, tokens_lo):
        sp = j % 2
        for k in range(16):
            sl_tt[sp].dma(
                "sync", TTBUF[sp][:, k * 512:(k + 1) * 512],
                ln1T_s[k * 128:(k + 1) * 128,
                       tokens_lo + j * 512: tokens_lo + (j + 1) * 512])

    def proj_wload(w_scratch):
        wt_last = 0
        for eh in range(2):
            for k in range(16):
                wt_last = sl_wh.dma(
                    "sync", WHB[eh][:, k * 1024:(k + 1) * 1024],
                    w_scratch[k * 128:(k + 1) * 128,
                              eh * 1024:(eh + 1) * 1024])
        return wt_last

    def proj_wstat(w_scratch, kind, tokens_lo, n_tt, pre=None):
        evac_hist = []
        mm_hist = []
        if pre is None:
            st_wt[0].wait("sync")
            st_wt[1].wait("sync")
            st_wt[0].wait("vector")
            st_wt[1].wait("vector")
            wt_last = proj_wload(w_scratch)
            proj_tt_read(0, tokens_lo)
        else:
            wt_last = pre

        def tt_read(j):
            proj_tt_read(j, tokens_lo)

        for tt in range(n_tt):
            if tt + 1 < n_tt:
                if mm_hist:
                    R.wait("sync", sMM, mm_hist[-1])
                tt_read(tt + 1)
            sl_tt[tt % 2].wait("tensor")
            R.wait("tensor", sl_wh.sem, wt_last)
            for eb in range(16):
                eh, e = eb // 8, eb % 8
                idx = len(evac_hist)
                if idx >= 4:
                    R.wait("tensor", sEV, evac_hist[idx - 4])
                pb = PS(idx % 4)
                for k in range(16):
                    R.mm("tensor", pb,
                         WHB[eh][:, k * 1024 + e * 128:
                                 k * 1024 + (e + 1) * 128],
                         TTBUF[tt % 2][:, k * 512:(k + 1) * 512],
                         start=(k == 0), stop=(k == 15),
                         sem=sMM if k == 15 else None)
                mm_hist.append(sMM.n)
                R.wait("vector", sMM, sMM.n)
                if kind == "Q":
                    dst = AR_ACT[:, eb * 2048 + tt * 512:
                                 eb * 2048 + (tt + 1) * 512]
                    R.op("vector", "tensor_copy", dst, pb, sem=sEV)
                    evac_hist.append(sEV.n)
                else:
                    sl = idx % 4
                    stg = AR_E[:, sl * 512:(sl + 1) * 512]
                    st_e[sl].wait("vector")
                    R.op("vector", "tensor_copy", stg, pb, sem=sEV)
                    evac_hist.append(sEV.n)
                    R.wait("sync", sEV, sEV.n)
                    st_e[sl].dma("sync",
                                 k_own[eb * 128:(eb + 1) * 128,
                                       tt * 512:(tt + 1) * 512], stg)
        return mm_hist[-1]

    q_last_mm = proj_wstat(wTq_s, "Q", 0, 4)
    q_done_ev = sEV.n

    R.wait("sync", sMM, q_last_mm)
    gwT_t = sl_wh.dma("sync", GWT[:, :], wTg_s[:, :])
    k_wt = proj_wload(wTk_s)
    proj_tt_read(0, 0)
    R.wait("tensor", sl_wh.sem, gwT_t)
    R.wait("tensor", sEV, q_done_ev)
    sl_kb.wait("scalar")
    g_sig = []
    for h in range(NH):
        for tt in range(4):
            i = h * 4 + tt
            pb = PS(i % 4)
            if i >= 4:
                R.wait("tensor", sEX, g_sig[i - 4])
            R.mm("tensor", pb, GWT[:, :],
                 AR_ACT[:, h * 2048 + tt * 512: h * 2048 + (tt + 1) * 512],
                 start=True, stop=True, sem=sMM)
            R.wait("scalar", sMM, sMM.n)
            sl = i % 4
            stg = AR_E[:, sl * 512:(sl + 1) * 512]
            st_e[sl].wait("scalar")
            R.op("scalar", "activation", stg, pb, AF.Sigmoid, bias=GB[:],
                 sem=sEX)
            g_sig.append(sEX.n)
            R.wait("sync", sEX, sEX.n)
            st_e[sl].dma("sync", g_s[h, :, tt * 512:(tt + 1) * 512], stg)
    barrier()
    if upto < 3:
        return finish(R, nc, es)

    proj_wstat(wTk_s, "K", 0, 4, pre=k_wt)
    for sl4 in st_e:
        sl4.wait("gpsimd")
    for ch in range(4):
        emit_gather(k_own[ch * 512:(ch + 1) * 512, :].opt(),
                    k_pair[ch * 1024:(ch + 1) * 1024, :].opt(),
                    cck[ch], PAIRS)
    barrier()
    if upto < 4:
        return finish(R, nc, es)

    v_evac = []
    v_mm = []
    VTB = [AR_D[:, 0:8192], AR_GAV[:, 0:8192]]
    VWB = [AR_W, AR_GAV[:, 16384:32768]]

    def v_tt_read(j):
        sp = j % 2
        for k in range(16):
            sl_tt[sp].dma("sync", VTB[sp][:, k * 512:(k + 1) * 512],
                          ln1T_s[k * 128:(k + 1) * 128,
                                 j * 512:(j + 1) * 512])

    vt_last = 0
    for eh in range(2):
        for k in range(16):
            vt_last = sl_wh.dma(
                "sync", VWB[eh][:, k * 1024:(k + 1) * 1024],
                wTv_s[k * 128:(k + 1) * 128,
                      eh * 1024:(eh + 1) * 1024])
    v_gt = {}
    v_tt_read(0)
    for tb in range(16):
        tt = tb // 4
        if tb % 2 == 0 and tb >= 2:
            ch = tb // 2 - 1
            for sl4 in st_e:
                sl4.wait("gpsimd")
            emit_gather(v_own[ch * 256:(ch + 1) * 256, :].opt(),
                        v_pair[ch * 512:(ch + 1) * 512, :].opt(),
                        ccv[ch], PAIRS)
        if tb % 4 == 0:
            if tt + 1 < 4:
                if v_mm:
                    R.wait("sync", sMM, v_mm[-1])
                v_tt_read(tt + 1)
            sl_tt[tt % 2].wait("tensor")
            R.wait("tensor", sl_wh.sem, vt_last)
        for eog in range(4):
            eh, eo = eog // 2, eog % 2
            idx = len(v_evac)
            if idx >= 4:
                R.wait("tensor", sEV, v_evac[idx - 4])
            pb = PS(idx % 4)
            for k in range(16):
                R.mm("tensor", pb,
                     VTB[tt % 2][:, k * 512 + (tb % 4) * 128:
                                 k * 512 + (tb % 4 + 1) * 128],
                     VWB[eh][:, k * 1024 + eo * 512:
                             k * 1024 + (eo + 1) * 512],
                     start=(k == 0), stop=(k == 15),
                     sem=sMM if k == 15 else None)
            v_mm.append(sMM.n)
            R.wait("vector", sMM, sMM.n)
            sl = idx % 4
            stg = AR_E[:, sl * 512:(sl + 1) * 512]
            st_e[sl].wait("vector")
            R.op("vector", "tensor_copy", stg, pb, sem=sEV)
            v_evac.append(sEV.n)
            R.wait("sync", sEV, sEV.n)
            st_e[sl].dma("sync",
                         v_own[tb * 128:(tb + 1) * 128,
                               eh * 1024 + eo * 512:
                               eh * 1024 + (eo + 1) * 512], stg)
    for sl4 in st_e:
        sl4.wait("gpsimd")
    emit_gather(v_own[1792:2048, :].opt(), v_pair[3584:4096, :].opt(),
                ccv[7], PAIRS)

    barrier()
    if upto < 5:
        return finish(R, nc, es)

    head_mm = []
    head_ev = []
    vp_views = [v_pair[ch * 512:ch * 512 + 256, :]
                .rearrange("(b k) (hh dd) -> hh k b dd", k=128, dd=128)
                for ch in range(8)]
    vo_view = v_own.rearrange("(b k) (hh dd) -> hh k b dd", k=128, dd=128)
    for sl4 in st_e:
        sl4.wait("sync")
    st_wt[0].wait("vector")
    st_wt[1].wait("vector")
    kv_t = {}

    pe3 = [None]

    def emit_head_loads(hh2):
        par2 = hh2 % 2
        if hh2 >= 2:
            R.wait("sync", sMM, head_mm[hh2 - 2])
            R.wait("sync", sEV, head_ev[hh2 - 2])
        if hh2 % 4 == 0:
            R.wait("sync", cck[hh2 // 4], 1)
        kt = sl_K[par2].dma(
            "sync", AR_W[:, par2 * 4096:par2 * 4096 + 2048],
            k_pair[(hh2 // 4) * 1024 + (hh2 % 4) * 128:
                   (hh2 // 4) * 1024 + (hh2 % 4 + 1) * 128, :])
        kt = sl_K[par2].dma("sync",
                            AR_W[:, par2 * 4096 + 2048:(par2 + 1) * 4096],
                            k_own[hh2 * 128:(hh2 + 1) * 128, :])
        if hh2 == 0:
            for ch in range(8):
                R.wait("sync", ccv[ch], 1)
        for vb in range(8):
            base = 8192 + par2 * 4096 + vb * 512
            if vb < 4:
                for sub in range(2):
                    vt = sl_V[par2].dma(
                        "sync",
                        AR_W[:, base + sub * 256:base + (sub + 1) * 256]
                        .rearrange("p (b dd) -> p b dd", dd=128),
                        vp_views[vb * 2 + sub][hh2][:, :, :])
            else:
                vt = sl_V[par2].dma(
                    "sync",
                    AR_W[:, base:base + 512]
                    .rearrange("p (b dd) -> p b dd", dd=128),
                    vo_view[hh2][:, (vb - 4) * 4:(vb - 3) * 4, :])
        gt = sl_G[par2].dma(
            "sync", AR_D[:, 4096 + par2 * 2048: 4096 + (par2 + 1) * 2048],
            g_s[hh2, :, :])
        kv_t[hh2] = (kt, sl_V[par2].sem.n, gt)

    emit_head_loads(0)
    emit_head_loads(1)
    for h in range(NH):
        par = h % 2
        k_t, v_t, g_t = kv_t[h]
        Kh = AR_W[:, par * 4096:(par + 1) * 4096]
        Vh = AR_W[:, 8192 + par * 4096: 8192 + (par + 1) * 4096]
        Gh = AR_D[:, 4096 + par * 2048: 4096 + (par + 1) * 2048]
        R.wait("tensor", sl_K[par].sem, k_t)
        R.wait("tensor", sl_V[par].sem, v_t)
        exp_done = {}
        av_done = {}
        mask_done = {}
        all_b = []
        for qt in range(4):
            nk = 20 + 4 * qt
            bs = []
            for lo in range(0, 16, 3):
                bs.append((lo, min(lo + 3, 16), 0))
            for lo in range(16, nk, 3):
                bs.append((lo, min(lo + 3, nk), 1))
            for bj, (cs, ce, reg) in enumerate(bs):
                all_b.append((qt, nk, cs, ce, reg, bj == 0,
                              bj == len(bs) - 1))

        def emit_av(gi):
            qt, nk, cs, ce, reg, first, last = all_b[gi]
            Ebase = (gi % 2) * 1536
            R.wait("tensor", sEX, exp_done[gi])
            if gi in mask_done:
                R.wait("tensor", sEV, mask_done[gi])
            if first and pe3[0] is not None:
                R.wait("tensor", sEX if False else sEV, pe3[0])
            for ci in range(ce - cs):
                c = cs + ci
                Esl = AR_D[:, Ebase + ci * 512:Ebase + (ci + 1) * 512]
                R.mm("tensor", PS(6), Vh[:, c * 128:(c + 1) * 128], Esl,
                     start=(c == 0), stop=(c == nk - 1))
                av_done[gi] = R.mm(
                    "tensor", PS(7), ONES[:, :], Esl,
                    start=(c == 0), stop=(c == nk - 1),
                    sem=sMM if ci == ce - cs - 1 else None)
            if last:
                last_av = av_done[gi]
                R.wait("scalar", sMM, last_av)
                rsf = AR_F[:, 4096:4608]
                R.op("scalar", "activation", rsf, PS(7), AF.Ln, sem=sEX)
                R.wait("scalar", sEX, sEX.n)
                R.op("scalar", "activation", rsf, rsf, AF.Exp, scale=-1.0,
                     sem=sEX)
                rs_t = sEX.n
                R.wait("vector", sEX, rs_t)
                R.wait("vector", sMM, last_av)
                R.wait("vector", sl_G[par].sem, g_t)
                avn = AR_F[:, 4608:5120]
                R.op("vector", "tensor_tensor", avn, PS(6), rsf,
                     op=ALU.mult, sem=sEV)
                R.wait("vector", sEV, sEV.n)
                R.op("vector", "tensor_tensor",
                     AR_GAV[:, h * 2048 + qt * 512:
                            h * 2048 + (qt + 1) * 512],
                     avn, Gh[:, qt * 512:(qt + 1) * 512], op=ALU.mult,
                     sem=sEV)
                pe3[0] = sEV.n

        for gi, (qt, nk, cs, ce, reg, first, last) in enumerate(all_b):
            blen = ce - cs
            sh = (gi % 2) * 3
            Ebase = (gi % 2) * 1536
            if gi >= 2:
                R.wait("tensor", sEX, exp_done[gi - 2])
            for ci in range(blen):
                R.mm("tensor", PS(sh + ci),
                     Kh[:, (cs + ci) * 128:(cs + ci + 1) * 128],
                     AR_ACT[:, h * 2048 + qt * 512:
                            h * 2048 + (qt + 1) * 512],
                     start=True, stop=True,
                     sem=sMM if ci == blen - 1 else None)
            qk_t = sMM.n
            R.wait("scalar", sMM, qk_t)
            if gi >= 2:
                R.wait("scalar", sMM, av_done[gi - 2])
            R.op("scalar", "activation",
                 AR_D[:, Ebase:Ebase + blen * 512], PS(sh, blen),
                 AF.Exp, scale=SCALE_QK, bias=KB[:, reg:reg + 1], sem=sEX)
            exp_done[gi] = sEX.n
            need_mask = [ci for ci in range(blen) if cs + ci >= nk - 4]
            if need_mask:
                R.wait("vector", sEX, exp_done[gi])
                for ci in need_mask:
                    j = cs + ci - (nk - 4)
                    sl = AR_D[:, Ebase + ci * 512:Ebase + (ci + 1) * 512]
                    R.op("vector", "tensor_tensor", sl, sl,
                         MASKS[:, j * 512:(j + 1) * 512],
                         op=ALU.mult, sem=sEV)
                mask_done[gi] = sEV.n
            if gi >= 1:
                emit_av(gi - 1)
        emit_av(len(all_b) - 1)
        head_mm.append(sMM.n)
        head_ev.append(sEV.n)
        if h == 2:
            st_wt[0].wait("gpsimd")
            st_wt[1].wait("gpsimd")
            R.wait("gpsimd", sMM, head_mm[1])
            emit_gather(wqgT_loc.opt(), wqgT_all.opt(), ccg, ALL8)
        if h == 7:
            R.wait("gpsimd", sMM, head_mm[6])
            emit_gather(wqpT_loc.opt(), wqpT_all.opt(), ccp, ALL8)
        if h + 2 < NH:
            emit_head_loads(h + 2)
    barrier()
    if upto < 6:
        return finish(R, nc, es)

    o_evac = []
    o_mm = []
    p4_at = [AR_D[:, 2048:4096], AR_D[:, 4096:6144], AR_D[:, 6144:8192]]
    p4_xt = [AR_F[:, 0:2048], AR_F[:, 2048:4096]]
    p4_sq = AR_D[:, 0:2048]
    x1ev_hist = {}
    x1st_hist = {}
    ev2_hist = {}
    ln2_tmm = {}
    ln2c_ev = {}

    def ln_stats2(src, c0, sem_eng_eps):
        s1 = STAT[:, c0:c0 + 1]
        s2 = STAT[:, c0 + 1:c0 + 2]
        negm = STAT[:, c0 + 2:c0 + 3]
        m2t = STAT[:, c0 + 3:c0 + 4]
        varr = STAT[:, c0 + 4:c0 + 5]
        rstd = STAT[:, c0 + 5:c0 + 6]
        R.op("vector", "scalar_tensor_tensor", p4_sq, src, 1.0, src,
             op0=ALU.mult, op1=ALU.mult, accum_out=s2, sem=sEV)
        R.op("vector", "reduce_sum", s1, src, axis=AX.X, sem=sEV)
        R.wait("vector", sEV, sEV.n)
        R.op("vector", "tensor_scalar", negm, s1, -1.0 / D, None,
             op0=ALU.mult, sem=sEV)
        R.wait("vector", sEV, sEV.n)
        R.op("vector", "tensor_scalar", m2t, negm, negm, None,
             op0=ALU.mult, sem=sEV)
        R.wait("vector", sEV, sEV.n)
        R.op("vector", "scalar_tensor_tensor", varr, s2, 1.0 / D, m2t,
             op0=ALU.mult, op1=ALU.subtract, sem=sEV)
        R.wait("scalar", sEV, sEV.n)
        R.op("scalar", "activation", rstd, varr, AF.Ln, bias=EPSL[:],
             sem=sEX)
        R.wait("scalar", sEX, sEX.n)
        R.op("scalar", "activation", rstd, rstd, AF.Exp, scale=-0.5, sem=sEX)
        return sEX.n

    LN2V = AR_ACT.rearrange("p (k c) -> p k c", c=2048)

    def emit_ln2T(qb):
        par = qb % 2
        PT = psum[:, 2048 + par * 1024:
                  2048 + (par + 1) * 1024].bitcast(BF16)
        if qb >= 2:
            R.wait("tensor", sEX, ln2c_ev[qb - 2])
        R.wait("tensor", sEV, ev2_hist[qb])
        src_at = p4_at[qb % 3]
        for k in range(16):
            R.op("tensor", "transpose", PT[:, k * 128:(k + 1) * 128],
                 src_at[:, k * 128:(k + 1) * 128], IDN[:],
                 sem=sMM if k == 15 else None)
        ln2_tmm[qb] = sMM.n
        R.wait("scalar", sMM, ln2_tmm[qb])
        R.op("scalar", "activation",
             LN2V[:, :, qb * 128:(qb + 1) * 128],
             PT.rearrange("p (k c) -> p k c", c=128), AF.Copy,
             sem=sEX)
        ln2c_ev[qb] = sEX.n

    def p4b_tile(qb):
        par = qb % 2
        at = p4_at[qb % 3]
        xt = p4_xt[par]
        c0 = 16 + par * 16
        c1 = c0 + 8
        if qb >= 2:
            emit_ln2T(qb - 2)
        if qb >= 3:
            R.wait("scalar", sMM, ln2_tmm[qb - 3])
        for sl4 in st_e:
            sl4.wait("scalar")
        a_t = sl_xa.dma("scalar", at, attn_s[qb * 128:(qb + 1) * 128, :])
        if qb >= 2:
            R.wait("scalar", sEV, ev2_hist[qb - 2])
            R.wait("scalar", st_x1[par].sem, x1st_hist[qb - 2])
        x_t = sl_xb.dma("scalar", xt,
                        x_in[2048 + qb * 128: 2048 + (qb + 1) * 128, :])
        R.wait("vector", sl_xa.sem, a_t)
        rt1 = ln_stats2(at, c0, None)
        rstd1 = STAT[:, c0 + 5:c0 + 6]
        negm1 = STAT[:, c0 + 2:c0 + 3]
        nmr = STAT[:, c0 + 6:c0 + 7]
        R.wait("vector", sEX, rt1)
        R.op("vector", "tensor_scalar", nmr, negm1, rstd1, None,
             op0=ALU.mult, sem=sEV)
        R.wait("vector", sEV, sEV.n)
        R.wait("vector", sl_xb.sem, x_t)
        R.op("vector", "tensor_scalar", xt, xt, nmr, None,
             op0=ALU.add, sem=sEV)
        R.wait("vector", sEV, sEV.n)
        R.op("vector", "scalar_tensor_tensor", xt, at, rstd1, xt,
             op0=ALU.mult, op1=ALU.add, sem=sEV)
        x1ev_hist[qb] = sEV.n
        R.wait("gpsimd", sEV, x1ev_hist[qb])
        x1st_hist[qb] = st_x1[par].dma("gpsimd",
                                       x1_s[qb * 128:(qb + 1) * 128, :], xt)
        R.wait("vector", sEV, x1ev_hist[qb])
        rt2 = ln_stats2(xt, c1, None)
        R.wait("vector", sEX, rt2)
        R.op("vector", "tensor_scalar", at, xt,
             STAT[:, c1 + 2:c1 + 3], STAT[:, c1 + 5:c1 + 6],
             op0=ALU.add, op1=ALU.mult, sem=sEV)
        ev2_hist[qb] = sEV.n

    for eoh in range(2):
        if eoh == 1:
            R.wait("sync", sMM, o_mm[-1])
        ot_last = 0
        for e in range(16):
            ot_last = sl_wh.dma(
                "sync", AR_W[:, e * 1024:(e + 1) * 1024],
                wTo_s[e * 128:(e + 1) * 128,
                      eoh * 1024:(eoh + 1) * 1024])
        R.wait("tensor", sl_wh.sem, ot_last)
        for qb in range(16):
            for eot in range(2):
                idx = len(o_evac)
                if idx >= 4:
                    R.wait("tensor", sEV, o_evac[idx - 4])
                pb = PS(idx % 4)
                for e in range(16):
                    R.mm("tensor", pb,
                         AR_GAV[:, e * 2048 + qb * 128:
                                e * 2048 + (qb + 1) * 128],
                         AR_W[:, e * 1024 + eot * 512:
                              e * 1024 + (eot + 1) * 512],
                         start=(e == 0), stop=(e == 15),
                         sem=sMM if e == 15 else None)
                o_mm.append(sMM.n)
                R.wait("vector", sMM, o_mm[-1])
                sl = idx % 4
                stg = AR_E[:, sl * 512:(sl + 1) * 512]
                st_e[sl].wait("vector")
                R.op("vector", "tensor_copy", stg, pb, sem=sEV)
                o_evac.append(sEV.n)
                R.wait("sync", sEV, o_evac[-1])
                st_e[sl].dma("sync",
                             attn_s[qb * 128:(qb + 1) * 128,
                                    eoh * 1024 + eot * 512:
                                    eoh * 1024 + (eot + 1) * 512], stg)
            if eoh == 1:
                p4b_tile(qb)
    emit_ln2T(14)
    emit_ln2T(15)
    if upto < 8:
        barrier()
        return finish(R, nc, es)

    hs_pool = []
    h_evac = []
    gu_mm = []
    gu_mm_pr = {}
    wq2_t = {}
    R.wait("sync", sMM, o_mm[-1])
    R.wait("sync", ccg, 1)
    R.wait("scalar", sEX, ln2c_ev[15])
    R.wait("gpsimd", sMM, ln2_tmm[15])

    def load_gu(pr):
        par = pr % 2
        rg, ru = pr // 4, 4 + pr // 4
        hl0 = (pr % 4) * 512
        if pr >= 2:
            R.wait("sync", sMM, gu_mm_pr[pr - 2])
        for kh in range(2):
            sl_wq2.dma(
                "sync",
                AR_GAV[:, par * 16384 + kh * 4096:
                       par * 16384 + (kh + 1) * 4096]
                .rearrange("p (k f) -> p k f", f=512),
                wqgT_all[rg * 2048 + kh * 1024: rg * 2048 + (kh + 1) * 1024,
                         hl0:hl0 + 512]
                .rearrange("(k p) f -> p k f", p=128))
        for kh in range(2):
            wq2_t[pr] = sl_wq2.dma(
                "sync",
                AR_GAV[:, par * 16384 + 8192 + kh * 4096:
                       par * 16384 + 8192 + (kh + 1) * 4096]
                .rearrange("p (k f) -> p k f", f=512),
                wqgT_all[ru * 2048 + kh * 1024: ru * 2048 + (kh + 1) * 1024,
                         hl0:hl0 + 512]
                .rearrange("(k p) f -> p k f", p=128))

    def emit_ones(j):
        hb63 = (j // 4 == 63)
        h2j = AR_D[:, 2048 + (j % 2) * 512: 2048 + (j % 2) * 512 + 512]
        R.wait("tensor", sPO, hs_pool[j])
        R.mm("tensor", PS(4 + j % 4), ONES[:, :], h2j,
             start=(j // 4 == 0), stop=hb63, sem=sMM if hb63 else None)

    load_gu(0)
    load_gu(1)
    R.wait("tensor", sEV, o_evac[-1])
    for pr in range(16):
        par = pr % 2
        R.wait("tensor", sl_wq2.sem, wq2_t[pr])
        for hb in range(4):
            hblk = pr * 4 + hb
            for tt in range(4):
                it = hblk * 4 + tt
                ba = (it % 2) * 2
                if pr == 0 and hb == 0:
                    R.wait("tensor", sEX, ln2c_ev[(tt + 1) * 4 - 1])
                if it >= 2:
                    R.wait("tensor", sEV, h_evac[it - 2])
                for k in range(16):
                    R.mm("tensor", PS(ba),
                         AR_GAV[:, par * 16384 + k * 512 + hb * 128:
                                par * 16384 + k * 512 + (hb + 1) * 128],
                         AR_ACT[:, k * 2048 + tt * 512:
                                k * 2048 + (tt + 1) * 512],
                         start=(k == 0), stop=(k == 15))
                for k in range(16):
                    R.mm("tensor", PS(ba + 1),
                         AR_GAV[:, par * 16384 + 8192 + k * 512 + hb * 128:
                                par * 16384 + 8192 + k * 512 + (hb + 1) * 128],
                         AR_ACT[:, k * 2048 + tt * 512:
                                k * 2048 + (tt + 1) * 512],
                         start=(k == 0), stop=(k == 15),
                         sem=sMM if k == 15 else None)
                gu_mm.append(sMM.n)
                if it >= 1:
                    emit_ones(it - 1)
                R.wait("scalar", sMM, gu_mm[-1])
                sg = AR_D[:, (it % 2) * 1024:(it % 2) * 1024 + 512]
                ht = AR_D[:, (it % 2) * 1024 + 512:(it % 2) * 1024 + 1024]
                h2 = AR_D[:, 2048 + (it % 2) * 512:
                          2048 + (it % 2) * 512 + 512]
                if it >= 2:
                    R.wait("scalar", sEV, h_evac[it - 2])
                R.op("scalar", "activation", sg, PS(ba), AF.Silu, sem=sEX)
                R.wait("vector", sEX, sEX.n)
                if it >= 2:
                    R.wait("vector", sPO, hs_pool[it - 2])
                    st_h[it % 2].wait("vector")
                R.op("vector", "tensor_tensor", ht, sg, PS(ba + 1),
                     op=ALU.mult, sem=sEV)
                h_evac.append(sEV.n)
                R.wait("gpsimd", sEV, sEV.n)
                R.op("gpsimd", "tensor_tensor", h2, ht, ht, op=ALU.mult,
                     sem=sPO)
                hs_pool.append(sPO.n)
                R.wait("sync", sEV, h_evac[-1])
                st_h[it % 2].dma("sync",
                                 h_s[hblk * 128:(hblk + 1) * 128,
                                     tt * 512:(tt + 1) * 512], ht)
        gu_mm_pr[pr] = sMM.n
        if pr + 2 < 16:
            load_gu(pr + 2)
    emit_ones(255)
    rms_mm = sMM.n
    barrier()
    if upto < 9:
        return finish(R, nc, es)

    R.wait("scalar", sMM, rms_mm)
    for tt in range(4):
        R.op("scalar", "activation", RRMS[:, tt * 512:(tt + 1) * 512],
             PS(4 + tt), AF.Ln, scale=1.0 / HH, bias=EPSR[:], sem=sEX)
        R.wait("scalar", sEX, sEX.n)
        R.op("scalar", "activation", RRMS[:, tt * 512:(tt + 1) * 512],
             RRMS[:, tt * 512:(tt + 1) * 512], AF.Exp, scale=-0.5, sem=sEX)
    rrms_t = sEX.n
    R.wait("vector", sEX, rrms_t)
    for sh2 in st_h:
        sh2.wait("sync")

    f_evac = []
    p_mm = []
    mid_mm = {}
    end_mm = {}
    hx_t = {}
    wp_t = {}

    def load_wp(g):
        r = g % 8
        for half in range(2):
            if g >= 1:
                R.wait("scalar", sMM,
                       mid_mm[g - 1] if half == 0 else end_mm[g - 1])
            for qq in range(2):
                o0 = half * 8192 + qq * 4096
                s0 = r * HH + half * 4096 + qq * 2048
                wp_t[(g, half)] = sl_wp[half].dma(
                    "scalar",
                    AR_W[:, o0:o0 + 4096]
                    .rearrange("p (hc e) -> p hc e", e=256),
                    wqpT_all[s0:s0 + 2048, 0:256]
                    .rearrange("(hc p) e -> p hc e", p=128))

    R.wait("tensor", sEX, rrms_t)
    R.wait("scalar", ccp, 1)
    load_wp(0)
    for ttx in range(2):
        if ttx == 1:
            R.wait("sync", sMM, end_mm[7])
        for hc in range(64):
            dstA = (AR_ACT[:, hc * 1024:(hc + 1) * 1024] if hc < 32 else
                    AR_GAV[:, (hc - 32) * 1024:(hc - 31) * 1024])
            hx_t[(ttx, hc)] = sl_hx.dma(
                "sync", dstA,
                h_s[hc * 128:(hc + 1) * 128, ttx * 1024:(ttx + 1) * 1024])
        for eo in range(4):
            for ebg in range(2):
                gidx = (ttx * 8 + eo * 2 + ebg)
                bb = (gidx % 2) * 4
                e0 = eo * 512 + ebg * 256
                if gidx >= 2:
                    R.wait("tensor", sEV, f_evac[4 * (gidx - 2) + 3])
                for hc in range(64):
                    half = hc // 32
                    if hc == 0:
                        R.wait("tensor", sl_wp[0].sem, wp_t[(gidx, 0)])
                    if hc == 32:
                        R.wait("tensor", sl_wp[1].sem, wp_t[(gidx, 1)])
                    R.wait("tensor", sl_hx.sem, hx_t[(ttx, hc)])
                    hsrc = (AR_ACT[:, hc * 1024:(hc + 1) * 1024] if hc < 32
                            else AR_GAV[:, (hc - 32) * 1024:(hc - 31) * 1024])
                    w0 = half * 8192 + (hc % 32) * 256
                    for eb_in in range(2):
                        for tt_in in range(2):
                            R.mm("tensor", PS(bb + eb_in * 2 + tt_in),
                                 AR_W[:, w0 + eb_in * 128:
                                      w0 + (eb_in + 1) * 128],
                                 hsrc[:, tt_in * 512:(tt_in + 1) * 512],
                                 start=(hc == 0), stop=(hc == 63),
                                 sem=sMM if (eb_in == 1 and tt_in == 1)
                                 else None)
                    if hc == 31:
                        mid_mm[gidx] = sMM.n
                end_mm[gidx] = sMM.n
                p_mm.append(sMM.n)
                if gidx + 1 < 16:
                    load_wp(gidx + 1)
                R.wait("vector", sMM, end_mm[gidx])
                for eb_in in range(2):
                    for tt_in in range(2):
                        sidx = len(f_evac)
                        sl = sidx % 4
                        stg = AR_D[:, sl * 512:(sl + 1) * 512]
                        st_f[sl].wait("vector")
                        R.op("vector", "tensor_tensor", stg,
                             PS(bb + eb_in * 2 + tt_in),
                             RRMS[:, (ttx * 2 + tt_in) * 512:
                                  (ttx * 2 + tt_in + 1) * 512],
                             op=ALU.mult, sem=sEV)
                        f_evac.append(sEV.n)
                        R.wait("sync", sEV, sEV.n)
                        r0 = e0 + eb_in * 128
                        st_f[sl].dma("sync",
                                     f_s[r0:r0 + 128,
                                         ttx * 1024 + tt_in * 512:
                                         ttx * 1024 + (tt_in + 1) * 512], stg)
    barrier()
    if upto < 10:
        return finish(R, nc, es)

    for sf in st_f:
        sf.wait("sync")
    st_x1[0].wait("sync")
    st_x1[1].wait("sync")
    p7_ev = {}
    p7_st = {}
    for qb in range(16):
        par = qb % 2
        ft = p4_at[par]
        xt = p4_xt[par]
        c0 = 16 + par * 16
        if qb >= 2:
            R.wait("sync", sEV, p7_ev[qb - 2])
        f_t = sl_ft.dma("sync", ft, f_s[:, qb * 128:(qb + 1) * 128],
                        transpose=True)
        if qb >= 2:
            R.wait("scalar", st_out[par].sem, p7_st[qb - 2])
        x_t = sl_xb.dma("scalar", xt, x1_s[qb * 128:(qb + 1) * 128, :])
        R.wait("vector", sl_ft.sem, f_t)
        rt = ln_stats2(ft, c0, None)
        rstd = STAT[:, c0 + 5:c0 + 6]
        negm = STAT[:, c0 + 2:c0 + 3]
        nmr = STAT[:, c0 + 6:c0 + 7]
        R.wait("vector", sEX, rt)
        R.op("vector", "tensor_scalar", nmr, negm, rstd, None,
             op0=ALU.mult, sem=sEV)
        R.wait("vector", sEV, sEV.n)
        R.wait("vector", sl_xb.sem, x_t)
        R.op("vector", "tensor_scalar", xt, xt, nmr, None,
             op0=ALU.add, sem=sEV)
        R.wait("vector", sEV, sEV.n)
        R.op("vector", "scalar_tensor_tensor", xt, ft, rstd, xt,
             op0=ALU.mult, op1=ALU.add, sem=sEV)
        p7_ev[qb] = sEV.n
        R.wait("gpsimd", sEV, p7_ev[qb])
        p7_st[qb] = st_out[par].dma("gpsimd",
                               out[qb * 128:(qb + 1) * 128, :], xt)
    st_out[0].wait("gpsimd")
    st_out[1].wait("gpsimd")
    barrier()
    st_out[0].wait("sync")
    st_out[1].wait("sync")
    return finish(R, nc, es)


def make_inputs(core, full):
    seq, half = core // 2, core % 2
    x = np.asarray(full["L0_attn_x"], dtype=np.float32)
    ctx = x[seq, 0:2048]
    own = x[seq, half * 2048:(half + 1) * 2048]
    x_core = np.ascontiguousarray(np.concatenate([ctx, own], axis=0))
    kb = np.empty((128, 2), np.float32)
    kb[:, 0] = (MSHIFT if half == 1 else -100000.0)
    kb[:, 1] = MSHIFT
    return {
        "x_in": x_core,
        "kb_in": kb,
        "gb_in": np.ascontiguousarray(np.asarray(full["L0_attn_gate_b"],
                                                 np.float32)),
        "q_w": np.ascontiguousarray(np.asarray(full["L0_attn_q_w"], np.float32)),
        "k_w": np.ascontiguousarray(np.asarray(full["L0_attn_k_w"], np.float32)),
        "v_w": np.ascontiguousarray(np.asarray(full["L0_attn_v_w"], np.float32)),
        "o_w": np.ascontiguousarray(np.asarray(full["L0_attn_o_w"], np.float32)),
        "gate_w": np.ascontiguousarray(np.asarray(full["L0_attn_gate_w"],
                                                  np.float32)),
        "w_gu": np.ascontiguousarray(np.asarray(full["L0_ffn_w_gate_up"],
                                                np.float32)
                                     [core * 2048:(core + 1) * 2048]),
        "w_pr": np.ascontiguousarray(np.asarray(full["L0_ffn_w_proj"],
                                                np.float32)
                                     [core * 256:(core + 1) * 256]),
    }


def kernel(**inputs):
    nc = build_program()
    in_maps = [make_inputs(c, inputs) for c in range(8)]
    res = run_bass_kernel_spmd(nc, in_maps, core_ids=list(range(8)))
    outp = np.empty((4, 4096, 2048), np.float32)
    for c in range(8):
        seq, half = c // 2, c % 2
        outp[seq, half * 2048:(half + 1) * 2048] = res.results[c]["out"]
    return outp


# version 32
# speedup vs baseline: 1.0212x; 1.0166x over previous
import numpy as np
import ml_dtypes
import concourse.bass as bass
import concourse.mybir as mybir
from concourse.bass_utils import run_bass_kernel_spmd
from contextlib import ExitStack

F32 = mybir.dt.float32
BF16 = mybir.dt.bfloat16
bf16 = ml_dtypes.bfloat16
AF = mybir.ActivationFunctionType
ALU = mybir.AluOpType
AX = mybir.AxisListType

ENGINES = ("tensor", "vector", "scalar", "gpsimd", "sync")

D = 2048
SEQ = 4096
TOWN = 2048
NH = 16
HF = 16384
HH = 8192
MSHIFT = -16.0
SCALE_QK = float(np.float32(1.0) / np.sqrt(np.float32(128.0)))
EPS_LN = 1e-5
EPS_RMS = float(np.finfo(np.float32).eps)


class S:
    def __init__(self, nc, es, name):
        self.h = es.enter_context(nc.semaphore(name))
        self.n = 0

    def bump(self, k):
        self.n += k
        return self.n


class Rec:
    def __init__(self):
        self.lists = {e: [] for e in ENGINES}

    def add(self, eng, fn):
        self.lists[eng].append(fn)

    def wait(self, eng, sem, val):
        if val <= 0:
            return
        self.add(eng, lambda q, s=sem.h, v=val: q.wait_ge(s, v))

    def op(self, eng, name, *args, sem=None, inc=1, **kw):
        tgt = sem.bump(inc) if sem is not None else None

        def f(q, nm=name, aa=args, kk=dict(kw), s=sem, ii=inc):
            inst = getattr(q, nm)(*aa, **kk)
            if s is not None:
                inst.then_inc(s.h, ii)

        self.add(eng, f)
        return tgt

    def mm(self, eng, out, lhsT, rhs, start, stop, sem=None, inc=1):
        tgt = sem.bump(inc) if sem is not None else None

        def f(q, o=out, a=lhsT, b=rhs, st=start, sp=stop, s=sem, ii=inc):
            inst = q.matmul(o, a, b, start=st, stop=sp)
            if s is not None:
                inst.then_inc(s.h, ii)

        self.add(eng, f)
        return tgt


class Slot:

    def __init__(self, R, nc, es, name):
        self.R = R
        self.sem = S(nc, es, name)

    def dma(self, eng, out, in_, **kw):
        tgt = self.sem.bump(16)

        def f(q, o=out, i=in_, s=self.sem.h, k=dict(kw)):
            q.dma_start(out=o, in_=i, **k).then_inc(s, 16)

        self.R.add(eng, f)
        return tgt

    def wait(self, eng, tgt=None):
        self.R.wait(eng, self.sem, self.sem.n if tgt is None else tgt)


def finish(R, nc, es):
    with nc.Block() as block:
        @block.tensor
        def _(q):
            for f in R.lists["tensor"]:
                f(q)

        @block.vector
        def _(q):
            for f in R.lists["vector"]:
                f(q)

        @block.scalar
        def _(q):
            for f in R.lists["scalar"]:
                f(q)

        @block.gpsimd
        def _(q):
            for f in R.lists["gpsimd"]:
                f(q)

        @block.sync
        def _(q):
            for f in R.lists["sync"]:
                f(q)

    es.close()
    return nc


def build_program(dbg=(), upto=99):
    dbg = set(dbg)
    nc = bass.Bass("TRN2", target_bir_lowering=False, debug=False)

    def dram(name, shape, dtype, kind="Internal"):
        k = "ExternalOutput" if (name in dbg and kind == "Internal") else kind
        return nc.dram_tensor(name, shape, dtype, kind=k).ap()

    x_in = dram("x_in", [SEQ, D], F32, "ExternalInput")
    kb_in = dram("kb_in", [128, 2], F32, "ExternalInput")
    gb_in = dram("gb_in", [128], F32, "ExternalInput")
    q_w = dram("q_w", [D, D], F32, "ExternalInput")
    k_w = dram("k_w", [D, D], F32, "ExternalInput")
    v_w = dram("v_w", [D, D], F32, "ExternalInput")
    o_w = dram("o_w", [D, D], F32, "ExternalInput")
    gate_w = dram("gate_w", [128, 128], F32, "ExternalInput")
    w_gu = dram("w_gu", [2048, D], F32, "ExternalInput")
    w_pr = dram("w_pr", [256, HH], F32, "ExternalInput")
    out = dram("out", [TOWN, D], F32, "ExternalOutput")

    ln1T_s = dram("ln1T_s", [D, TOWN], BF16)
    wTq_s = dram("wTq_s", [D, D], BF16)
    wTk_s = dram("wTk_s", [D, D], BF16)
    wTv_s = dram("wTv_s", [D, D], BF16)
    wTo_s = dram("wTo_s", [D, D], BF16)
    wTg_s = dram("wTg_s", [128, 128], BF16)
    k_own = dram("k_own", [NH * 128, TOWN], BF16)
    k_pair = dram("k_pair", [2 * NH * 128, TOWN], BF16)
    v_own = dram("v_own", [TOWN, D], BF16)
    v_pair = dram("v_pair", [2 * TOWN, D], BF16)
    g_s = dram("g_s", [NH, 128, TOWN], BF16)
    attn_s = dram("attn_s", [TOWN, D], BF16)
    x1_s = dram("x1_s", [TOWN, D], F32)
    wqgT_loc = dram("wqgT_loc", [D, 2048], BF16)
    wqpT_loc = dram("wqpT_loc", [HH, 256], BF16)
    wqgT_all = dram("wqgT_all", [8 * D, 2048], BF16)
    wqpT_all = dram("wqpT_all", [8 * HH, 256], BF16)
    h_s = dram("h_s", [HH, TOWN], BF16)
    f_s = dram("f_s", [D, TOWN], BF16)

    es = ExitStack()

    def sbuf(name, shape, dtype):
        return es.enter_context(nc.sbuf_tensor(name, shape, dtype))

    AR_W = sbuf("AR_W", [128, 16384], BF16)
    AR_ACT = sbuf("AR_ACT", [128, 32768], BF16)
    AR_GAV = sbuf("AR_GAV", [128, 32768], BF16)
    AR_D = sbuf("AR_D", [128, 8192], BF16)
    AR_E = sbuf("AR_E", [128, 2048], BF16)
    AR_F = sbuf("AR_F", [128, 5120], F32)
    STAT = sbuf("STAT", [128, 64], F32)
    QST = sbuf("QST", [128, 128], F32)
    KB = sbuf("KB", [128, 2], F32)
    GB = sbuf("GB", [128, 1], F32)
    EPSL = sbuf("EPSL", [128, 1], F32)
    EPSR = sbuf("EPSR", [128, 1], F32)
    ONES = sbuf("ONES", [128, 128], BF16)
    GWT = sbuf("GWT", [128, 128], BF16)
    IDN = sbuf("IDN", [128, 128], BF16)
    MASKS = sbuf("MASKS", [128, 4 * 512], BF16)
    SCB = sbuf("SCB", [128, 32], BF16)

    RRMS = AR_F[:, 0:2048]
    psum = es.enter_context(nc.psum_tensor("psAll", [128, 4096], F32))

    def PS(b, n=1):
        return psum[:, b * 512:(b + n) * 512]

    R = Rec()
    sems = {k: S(nc, es, k) for k in ("bar", "sMM", "sEV", "sEX", "sPO")}
    bar, sMM, sEV, sEX, sPO = (sems[k] for k in
                               ("bar", "sMM", "sEV", "sEX", "sPO"))
    ccg = S(nc, es, "ccg")
    ccp = S(nc, es, "ccp")
    cck = [S(nc, es, f"cck{i}") for i in range(4)]
    ccv = [S(nc, es, f"ccv{i}") for i in range(8)]

    def SL(name):
        return Slot(R, nc, es, name)

    sl_kb = SL("l_kb")
    sl_xa = SL("l_xa")
    sl_xb = SL("l_xb")
    sl_wc = [SL("l_wc0"), SL("l_wc1")]
    sl_tt = [SL("l_tt0"), SL("l_tt1")]
    sl_wh = SL("l_wh")
    sl_K = [SL("l_K0"), SL("l_K1")]
    sl_V = [SL("l_V0"), SL("l_V1")]
    sl_G = [SL("l_G0"), SL("l_G1")]
    sl_hx = SL("l_hx")
    sl_wp = [SL("l_wp0"), SL("l_wp1")]
    sl_qs = [SL("l_qs0"), SL("l_qs1")]
    sl_ft = SL("l_ft")
    sl_wq2 = [SL("l_wq20"), SL("l_wq21")]
    sl_wt = [SL("l_wt0"), SL("l_wt1")]
    st_wt = [SL("s_wt0"), SL("s_wt1")]
    st_ln1 = [SL("s_ln10"), SL("s_ln11")]
    st_wc = [SL("s_wc0"), SL("s_wc1")]
    st_e = [SL("s_e0"), SL("s_e1"), SL("s_e2"), SL("s_e3")]
    st_wq = SL("s_wq")
    st_x1 = [SL("s_x10"), SL("s_x11")]
    st_ln2 = SL("s_ln2")
    st_h = [SL("s_h0"), SL("s_h1")]
    st_f = [SL("s_f0"), SL("s_f1"), SL("s_f2"), SL("s_f3")]
    st_out = [SL("s_out0"), SL("s_out1")]

    def barrier():
        for e in ENGINES:
            R.add(e, lambda q, s=bar.h: q.nop().then_inc(s, 1))
        bar.n += 5
        for e in ENGINES:
            R.wait(e, bar, bar.n)

    def ln_stats(src, sb, sq_dst):
        s1 = STAT[:, sb + 0:sb + 1]
        s2 = STAT[:, sb + 1:sb + 2]
        negm = STAT[:, sb + 2:sb + 3]
        m2t = STAT[:, sb + 3:sb + 4]
        varr = STAT[:, sb + 4:sb + 5]
        rstd = STAT[:, sb + 5:sb + 6]
        R.op("vector", "scalar_tensor_tensor", sq_dst, src, 1.0, src,
             op0=ALU.mult, op1=ALU.mult, accum_out=s2, sem=sEV)
        R.op("vector", "reduce_sum", s1, src, axis=AX.X, sem=sEV)
        R.wait("vector", sEV, sEV.n)
        R.op("vector", "tensor_scalar", negm, s1, -1.0 / D, None,
             op0=ALU.mult, sem=sEV)
        R.wait("vector", sEV, sEV.n)
        R.op("vector", "tensor_scalar", m2t, negm, negm, None,
             op0=ALU.mult, sem=sEV)
        R.wait("vector", sEV, sEV.n)
        R.op("vector", "scalar_tensor_tensor", varr, s2, 1.0 / D, m2t,
             op0=ALU.mult, op1=ALU.subtract, sem=sEV)
        R.wait("scalar", sEV, sEV.n)
        R.op("scalar", "activation", rstd, varr, AF.Ln, bias=EPSL[:],
             sem=sEX)
        R.wait("scalar", sEX, sEX.n)
        R.op("scalar", "activation", rstd, rstd, AF.Exp, scale=-0.5, sem=sEX)
        return sEX.n

    def ln_fin(src, dst, sb, rstd_t):
        R.wait("vector", sEX, rstd_t)
        R.op("vector", "tensor_scalar", dst, src,
             STAT[:, sb + 2:sb + 3], STAT[:, sb + 5:sb + 6],
             op0=ALU.add, op1=ALU.mult, sem=sEV)
        return sEV.n

    q_ev = {}
    q_st = {}
    q_tmm = {}

    def quant_block(j, src, dst):
        G = 32
        A = AR_GAV[:, 8192 + (j % 2) * 2048: 8192 + (j % 2) * 2048 + 2048]
        B = AR_GAV[:, (j % 2) * 4096: (j % 2) * 4096 + 2048]
        C = AR_GAV[:, (j % 2) * 4096 + 2048: (j % 2) * 4096 + 4096]
        gsumt = QST[:, 0:G]
        sct = QST[:, G:2 * G]
        hst = QST[:, 2 * G:3 * G]
        nhst = QST[:, 3 * G:4 * G]
        scb = SCB[:, 0:G]
        if j >= 2:
            R.wait("gpsimd", sEV, q_ev[j - 2])
        ca_t = sl_qs[j % 2].dma("gpsimd", A, src)
        R.wait("vector", sl_qs[j % 2].sem, ca_t)
        R.op("vector", "tensor_reduce", gsumt,
             A.rearrange("p (g k) -> p g k", k=64), axis=AX.X, op=ALU.add,
             apply_absolute_value=True, sem=sEV)
        R.wait("vector", sEV, sEV.n)
        R.op("vector", "tensor_scalar", scb, gsumt, 1.0 / 64, 1e-8,
             op0=ALU.mult, op1=ALU.max, sem=sEV)
        R.wait("vector", sEV, sEV.n)
        R.op("vector", "tensor_copy", sct, scb, sem=sEV)
        R.wait("vector", sEV, sEV.n)
        R.op("vector", "tensor_scalar", hst, sct, 0.5, None, op0=ALU.mult,
             sem=sEV)
        R.op("vector", "tensor_scalar", nhst, sct, -0.5, None, op0=ALU.mult,
             sem=sEV)
        R.wait("vector", sEV, sEV.n)
        if j >= 2:
            R.wait("vector", sMM, q_tmm[j - 2])
        R.op("vector", "tensor_tensor",
             B.rearrange("p (g k) -> p g k", k=64),
             A.rearrange("p (g k) -> p g k", k=64),
             hst[:, :, None].broadcast_to([128, G, 64]), op=ALU.is_gt,
             sem=sEV)
        R.op("vector", "tensor_tensor",
             C.rearrange("p (g k) -> p g k", k=64),
             A.rearrange("p (g k) -> p g k", k=64),
             nhst[:, :, None].broadcast_to([128, G, 64]), op=ALU.is_lt,
             sem=sEV)
        ev_m = sEV.n
        R.wait("gpsimd", sEV, ev_m)
        R.op("gpsimd", "tensor_tensor", A, B, C, op=ALU.subtract, sem=sPO)
        R.wait("vector", sPO, sPO.n)
        R.op("vector", "tensor_tensor",
             B.rearrange("p (g k) -> p g k", k=64),
             A.rearrange("p (g k) -> p g k", k=64),
             scb[:, :, None].broadcast_to([128, G, 64]), op=ALU.mult, sem=sEV)
        q_ev[j] = sEV.n
        R.wait("tensor", sEV, q_ev[j])
        q_tmm[j], q_st[j] = pe_T(B, 2048, dst)
        return q_st[j]

    qjobs = []
    for b in range(16):
        qjobs.append((w_gu[b * 128:(b + 1) * 128, :],
                      wqgT_loc[0:2048, b * 128:(b + 1) * 128]))
    for rb in range(2):
        for cb in range(4):
            qjobs.append((w_pr[rb * 128:(rb + 1) * 128,
                               cb * 2048:(cb + 1) * 2048],
                          wqpT_loc[cb * 2048:(cb + 1) * 2048,
                                   rb * 128:(rb + 1) * 128]))

    sl_kb.dma("sync", KB[:], kb_in[:])
    sl_kb.dma("sync", GB[:], gb_in[:, None])
    R.op("gpsimd", "memset", ONES[:], 1.0, sem=sPO)
    R.op("gpsimd", "memset", IDN[:], 1.0, sem=sPO)
    R.op("gpsimd", "memset", MASKS[:], 1.0, sem=sPO)
    R.wait("gpsimd", sPO, sPO.n)
    for j in range(4):
        R.op("gpsimd", "affine_select", MASKS[:, j * 512:(j + 1) * 512],
             MASKS[:, j * 512:(j + 1) * 512], pattern=[[1, 512]],
             compare_op=ALU.is_ge, fill=0.0, base=-128 * j,
             channel_multiplier=-1, sem=sPO)
    R.op("gpsimd", "affine_select", IDN[:], IDN[:], pattern=[[1, 128]],
         compare_op=ALU.is_ge, fill=0.0, base=0,
         channel_multiplier=-1, sem=sPO)
    R.wait("gpsimd", sPO, sPO.n)
    R.op("gpsimd", "affine_select", IDN[:], IDN[:], pattern=[[-1, 128]],
         compare_op=ALU.is_ge, fill=0.0, base=0,
         channel_multiplier=1, sem=sPO)
    idn_tgt = sPO.n
    R.wait("tensor", sPO, idn_tgt)
    R.op("vector", "memset", EPSL[:], EPS_LN, sem=sEV)
    R.op("vector", "memset", EPSR[:], EPS_RMS, sem=sEV)

    pt_state = {"j": 0, "ev": {}, "st": {}, "mm": {}, "slot": {}}

    def pe_T(src, ncol, dst_col, tsg=None, lag=2):
        j = pt_state["j"]
        pt_state["j"] += 1
        par = j % 2
        nt = ncol // 128
        PT = psum[:, 2048 + par * 1024:
                  2048 + (par + 1) * 1024].bitcast(BF16)
        TSG = (AR_GAV[:, 12288 + par * 2048: 12288 + par * 2048 + 2048]
               if tsg is None else tsg)
        if j >= 2:
            R.wait("tensor", sEX, pt_state["ev"][j - 2])
        for k in range(nt):
            R.op("tensor", "transpose", PT[:, k * 128:(k + 1) * 128],
                 src[:, k * 128:(k + 1) * 128], IDN[:],
                 sem=sMM if k == nt - 1 else None)
        pt_state["mm"][j] = sMM.n
        R.wait("scalar", sMM, pt_state["mm"][j])
        if j >= lag and (j - lag) in pt_state["st"]:
            stp = pt_state["slot"][j - lag]
            R.wait("scalar", stp.sem, pt_state["st"][j - lag])
        R.op("scalar", "activation", TSG[:, 0:ncol], PT[:, 0:ncol], AF.Copy,
             sem=sEX)
        pt_state["ev"][j] = sEX.n
        R.wait("scalar", sEX, pt_state["ev"][j])
        for c0 in range(0, ncol, 1024):
            seg = min(1024, ncol - c0)
            pt_state["st"][j] = st_wt[par].dma(
                "scalar",
                dst_col[c0:c0 + seg, :].rearrange("(k p) c -> p k c", p=128),
                TSG[:, c0:c0 + seg].rearrange("p (k c) -> p k c", c=128))
        pt_state["slot"][j] = st_wt[par]
        return pt_state["mm"][j], pt_state["st"][j]

    wcast = []
    for wsrc, wdstT in ((q_w, wTq_s), (k_w, wTk_s), (v_w, wTv_s),
                        (o_w, wTo_s)):
        for blk in range(16):
            wcast.append((wsrc[blk * 128:(blk + 1) * 128, :],
                          wdstT[0:2048, blk * 128:(blk + 1) * 128], 2048))
    wcast.append((gate_w[:, :], wTg_s[0:128, 0:128], 128))
    wc_n = [0]
    wc_tmm = {}

    def emit_wcast():
        j = wc_n[0]
        if j >= len(wcast):
            return
        wc_n[0] += 1
        src, dstT, ncol = wcast[j]
        par = j % 2
        stg = AR_D[:, par * 2048:par * 2048 + ncol]
        if j >= 2:
            R.wait("gpsimd", sMM, wc_tmm[j - 2])
        t = sl_wc[par].dma("gpsimd", stg, src)
        R.wait("tensor", sl_wc[par].sem, t)
        wc_tmm[j], _ = pe_T(stg, ncol, dstT)

    apply_hist = {}
    store_hist = {}
    xsl = [sl_xa, sl_xb]
    XP = [AR_ACT[:, 0:8192].bitcast(F32), AR_ACT[:, 8192:16384].bitcast(F32)]
    LOUT2 = [AR_ACT[:, 16384:20480], AR_ACT[:, 20480:24576]]
    SQ1 = AR_ACT[:, 24576:26624]

    q_store = {}
    lout_tmm = {}
    qj = [0]
    for p in range(8):
        par = p % 2
        xp = XP[par]
        c = par * 16
        if p >= 2:
            R.wait("sync", sEX, apply_hist[p - 2])
        for t in range(2):
            ld = xsl[par].dma(
                "sync", xp[:, t * 2048:(t + 1) * 2048],
                x_in[(16 + 2 * p + t) * 128:(16 + 2 * p + t + 1) * 128, :])
        for _ in range(9):
            emit_wcast()
        s2p = STAT[:, c:c + 2]
        s1p = STAT[:, c + 2:c + 4]
        ngp = STAT[:, c + 4:c + 6]
        m2p = STAT[:, c + 6:c + 8]
        vrp = STAT[:, c + 8:c + 10]
        rsp = STAT[:, c + 10:c + 12]
        R.wait("vector", xsl[par].sem, ld)
        for t in range(2):
            R.op("vector", "scalar_tensor_tensor",
                 AR_ACT[:, 24576 + t * 2048:26624 + t * 2048],
                 xp[:, t * 2048:(t + 1) * 2048], 1.0,
                 xp[:, t * 2048:(t + 1) * 2048],
                 op0=ALU.mult, op1=ALU.mult,
                 accum_out=s2p[:, t:t + 1], sem=sEV)
            R.op("vector", "reduce_sum", s1p[:, t:t + 1],
                 xp[:, t * 2048:(t + 1) * 2048], axis=AX.X, sem=sEV)
        R.wait("vector", sEV, sEV.n)
        R.op("vector", "tensor_scalar", ngp, s1p, -1.0 / D, None,
             op0=ALU.mult, sem=sEV)
        R.wait("vector", sEV, sEV.n)
        R.op("vector", "tensor_tensor", m2p, ngp, ngp, op=ALU.mult, sem=sEV)
        R.wait("vector", sEV, sEV.n)
        R.op("vector", "scalar_tensor_tensor", vrp, s2p, 1.0 / D, m2p,
             op0=ALU.mult, op1=ALU.subtract, sem=sEV)
        R.wait("scalar", sEV, sEV.n)
        R.op("scalar", "activation", rsp, vrp, AF.Ln, bias=EPSL[:], sem=sEX)
        R.wait("scalar", sEX, sEX.n)
        R.op("scalar", "activation", rsp, rsp, AF.Exp, scale=-0.5, sem=sEX)
        R.wait("vector", sEX, sEX.n)
        nmrp = STAT[:, c + 12:c + 14]
        if p >= 2:
            R.wait("vector", sEX, apply_hist[p - 2])
        R.op("vector", "tensor_tensor", nmrp, ngp, rsp, op=ALU.mult, sem=sEV)
        R.wait("scalar", sEV, sEV.n)
        if p >= 2:
            R.wait("scalar", sMM, lout_tmm[p - 2])
        for t in range(2):
            R.op("scalar", "activation", LOUT2[par][:, t * 2048:
                                                    (t + 1) * 2048],
                 xp[:, t * 2048:(t + 1) * 2048], AF.Identity,
                 scale=STAT[:, c + 10 + t:c + 11 + t],
                 bias=STAT[:, c + 12 + t:c + 13 + t], sem=sEX)
        apply_hist[p] = sEX.n
        R.wait("tensor", sEX, apply_hist[p])
        for t in range(2):
            lout_tmm[p], _ = pe_T(
                LOUT2[par][:, t * 2048:(t + 1) * 2048], 2048,
                ln1T_s[0:2048, (2 * p + t) * 128:(2 * p + t + 1) * 128])
        for _ in range(3):
            q_store[qj[0]] = quant_block(qj[0], *qjobs[qj[0]])
            qj[0] += 1
    while wc_n[0] < len(wcast):
        emit_wcast()
    barrier()
    if upto < 2:
        return finish(R, nc, es)


    def emit_gather(in_ap, out_ap, sem, groups):
        def f(q, s=sem.h, a=in_ap, b=out_ap, g=groups):
            q.collective_compute(
                "AllGather", ALU.bypass,
                replica_groups=g,
                ins=[a], outs=[b]).then_inc(s)

        R.add("gpsimd", f)
        sem.n = 1

    PAIRS = [[0, 1], [2, 3], [4, 5], [6, 7]]
    ALL8 = [[0, 1, 2, 3, 4, 5, 6, 7]]

    TTBUF = [AR_D[:, 0:8192], AR_GAV[:, 0:8192]]
    WHB = [AR_W, AR_GAV[:, 16384:32768]]

    def proj_tt_read(j, tokens_lo):
        sp = j % 2
        for k in range(16):
            sl_tt[sp].dma(
                "sync", TTBUF[sp][:, k * 512:(k + 1) * 512],
                ln1T_s[k * 128:(k + 1) * 128,
                       tokens_lo + j * 512: tokens_lo + (j + 1) * 512])

    def proj_wload(w_scratch):
        wt_last = 0
        for eh in range(2):
            for k in range(16):
                wt_last = sl_wh.dma(
                    "sync", WHB[eh][:, k * 1024:(k + 1) * 1024],
                    w_scratch[k * 128:(k + 1) * 128,
                              eh * 1024:(eh + 1) * 1024])
        return wt_last

    def proj_wstat(w_scratch, kind, tokens_lo, n_tt, pre=None):
        evac_hist = []
        mm_hist = []
        if pre is None:
            st_wt[0].wait("sync")
            st_wt[1].wait("sync")
            st_wt[0].wait("vector")
            st_wt[1].wait("vector")
            wt_last = proj_wload(w_scratch)
            proj_tt_read(0, tokens_lo)
        else:
            wt_last = pre

        def tt_read(j):
            proj_tt_read(j, tokens_lo)

        for tt in range(n_tt):
            if tt + 1 < n_tt:
                if mm_hist:
                    R.wait("sync", sMM, mm_hist[-1])
                tt_read(tt + 1)
            sl_tt[tt % 2].wait("tensor")
            R.wait("tensor", sl_wh.sem, wt_last)
            for eb in range(16):
                eh, e = eb // 8, eb % 8
                idx = len(evac_hist)
                if idx >= 4:
                    R.wait("tensor", sEV, evac_hist[idx - 4])
                pb = PS(idx % 4)
                for k in range(16):
                    R.mm("tensor", pb,
                         WHB[eh][:, k * 1024 + e * 128:
                                 k * 1024 + (e + 1) * 128],
                         TTBUF[tt % 2][:, k * 512:(k + 1) * 512],
                         start=(k == 0), stop=(k == 15),
                         sem=sMM if k == 15 else None)
                mm_hist.append(sMM.n)
                R.wait("vector", sMM, sMM.n)
                if kind == "Q":
                    dst = AR_ACT[:, eb * 2048 + tt * 512:
                                 eb * 2048 + (tt + 1) * 512]
                    R.op("vector", "tensor_copy", dst, pb, sem=sEV)
                    evac_hist.append(sEV.n)
                else:
                    sl = idx % 4
                    stg = AR_E[:, sl * 512:(sl + 1) * 512]
                    st_e[sl].wait("vector")
                    R.op("vector", "tensor_copy", stg, pb, sem=sEV)
                    evac_hist.append(sEV.n)
                    R.wait("sync", sEV, sEV.n)
                    st_e[sl].dma("sync",
                                 k_own[eb * 128:(eb + 1) * 128,
                                       tt * 512:(tt + 1) * 512], stg)
        return mm_hist[-1]

    q_last_mm = proj_wstat(wTq_s, "Q", 0, 4)
    q_done_ev = sEV.n

    R.wait("sync", sMM, q_last_mm)
    gwT_t = sl_kb.dma("sync", GWT[:, :], wTg_s[:, :])
    k_wt = proj_wload(wTk_s)
    proj_tt_read(0, 0)
    R.wait("tensor", sl_kb.sem, gwT_t)
    R.wait("tensor", sEV, q_done_ev)
    sl_kb.wait("scalar")
    g_sig = []
    for h in range(NH):
        for tt in range(4):
            i = h * 4 + tt
            pb = PS(i % 4)
            if i >= 4:
                R.wait("tensor", sEX, g_sig[i - 4])
            R.mm("tensor", pb, GWT[:, :],
                 AR_ACT[:, h * 2048 + tt * 512: h * 2048 + (tt + 1) * 512],
                 start=True, stop=True, sem=sMM)
            R.wait("scalar", sMM, sMM.n)
            sl = i % 4
            stg = AR_E[:, sl * 512:(sl + 1) * 512]
            st_e[sl].wait("scalar")
            R.op("scalar", "activation", stg, pb, AF.Sigmoid, bias=GB[:],
                 sem=sEX)
            g_sig.append(sEX.n)
            R.wait("sync", sEX, sEX.n)
            st_e[sl].dma("sync", g_s[h, :, tt * 512:(tt + 1) * 512], stg)
    barrier()
    if upto < 3:
        return finish(R, nc, es)

    proj_wstat(wTk_s, "K", 0, 4, pre=k_wt)
    for sl4 in st_e:
        sl4.wait("gpsimd")
    for ch in range(4):
        emit_gather(k_own[ch * 512:(ch + 1) * 512, :].opt(),
                    k_pair[ch * 1024:(ch + 1) * 1024, :].opt(),
                    cck[ch], PAIRS)
    barrier()
    if upto < 4:
        return finish(R, nc, es)

    v_evac = []
    v_mm = []
    VTB = [AR_D[:, 0:8192], AR_GAV[:, 0:8192]]
    VWB = [AR_W, AR_GAV[:, 16384:32768]]

    def v_tt_read(j):
        sp = j % 2
        for k in range(16):
            sl_tt[sp].dma("sync", VTB[sp][:, k * 512:(k + 1) * 512],
                          ln1T_s[k * 128:(k + 1) * 128,
                                 j * 512:(j + 1) * 512])

    vt_last = 0
    for eh in range(2):
        for k in range(16):
            vt_last = sl_wh.dma(
                "sync", VWB[eh][:, k * 1024:(k + 1) * 1024],
                wTv_s[k * 128:(k + 1) * 128,
                      eh * 1024:(eh + 1) * 1024])
    v_gt = {}
    v_tt_read(0)
    for tb in range(16):
        tt = tb // 4
        if tb % 2 == 0 and tb >= 2:
            ch = tb // 2 - 1
            for sl4 in st_e:
                sl4.wait("gpsimd")
            emit_gather(v_own[ch * 256:(ch + 1) * 256, :].opt(),
                        v_pair[ch * 512:(ch + 1) * 512, :].opt(),
                        ccv[ch], PAIRS)
        if tb % 4 == 0:
            if tt + 1 < 4:
                if v_mm:
                    R.wait("sync", sMM, v_mm[-1])
                v_tt_read(tt + 1)
            sl_tt[tt % 2].wait("tensor")
            R.wait("tensor", sl_wh.sem, vt_last)
        for eog in range(4):
            eh, eo = eog // 2, eog % 2
            idx = len(v_evac)
            if idx >= 4:
                R.wait("tensor", sEV, v_evac[idx - 4])
            pb = PS(idx % 4)
            for k in range(16):
                R.mm("tensor", pb,
                     VTB[tt % 2][:, k * 512 + (tb % 4) * 128:
                                 k * 512 + (tb % 4 + 1) * 128],
                     VWB[eh][:, k * 1024 + eo * 512:
                             k * 1024 + (eo + 1) * 512],
                     start=(k == 0), stop=(k == 15),
                     sem=sMM if k == 15 else None)
            v_mm.append(sMM.n)
            R.wait("vector", sMM, sMM.n)
            sl = idx % 4
            stg = AR_E[:, sl * 512:(sl + 1) * 512]
            st_e[sl].wait("vector")
            R.op("vector", "tensor_copy", stg, pb, sem=sEV)
            v_evac.append(sEV.n)
            R.wait("sync", sEV, sEV.n)
            st_e[sl].dma("sync",
                         v_own[tb * 128:(tb + 1) * 128,
                               eh * 1024 + eo * 512:
                               eh * 1024 + (eo + 1) * 512], stg)
    for sl4 in st_e:
        sl4.wait("gpsimd")
    emit_gather(v_own[1792:2048, :].opt(), v_pair[3584:4096, :].opt(),
                ccv[7], PAIRS)

    barrier()
    if upto < 5:
        return finish(R, nc, es)

    head_mm = []
    head_ev = []
    vp_views = [v_pair[ch * 512:ch * 512 + 256, :]
                .rearrange("(b k) (hh dd) -> hh k b dd", k=128, dd=128)
                for ch in range(8)]
    vo_view = v_own.rearrange("(b k) (hh dd) -> hh k b dd", k=128, dd=128)
    for sl4 in st_e:
        sl4.wait("sync")
    st_wt[0].wait("vector")
    st_wt[1].wait("vector")
    kv_t = {}

    pe3 = [None]

    def emit_head_loads(hh2):
        par2 = hh2 % 2
        if hh2 >= 2:
            R.wait("sync", sMM, head_mm[hh2 - 2])
            R.wait("sync", sEV, head_ev[hh2 - 2])
        if hh2 % 4 == 0:
            R.wait("sync", cck[hh2 // 4], 1)
        kt = sl_K[par2].dma(
            "sync", AR_W[:, par2 * 4096:par2 * 4096 + 2048],
            k_pair[(hh2 // 4) * 1024 + (hh2 % 4) * 128:
                   (hh2 // 4) * 1024 + (hh2 % 4 + 1) * 128, :])
        kt = sl_K[par2].dma("sync",
                            AR_W[:, par2 * 4096 + 2048:(par2 + 1) * 4096],
                            k_own[hh2 * 128:(hh2 + 1) * 128, :])
        if hh2 == 0:
            for ch in range(8):
                R.wait("sync", ccv[ch], 1)
        for vb in range(8):
            base = 8192 + par2 * 4096 + vb * 512
            if vb < 4:
                for sub in range(2):
                    vt = sl_V[par2].dma(
                        "sync",
                        AR_W[:, base + sub * 256:base + (sub + 1) * 256]
                        .rearrange("p (b dd) -> p b dd", dd=128),
                        vp_views[vb * 2 + sub][hh2][:, :, :])
            else:
                vt = sl_V[par2].dma(
                    "sync",
                    AR_W[:, base:base + 512]
                    .rearrange("p (b dd) -> p b dd", dd=128),
                    vo_view[hh2][:, (vb - 4) * 4:(vb - 3) * 4, :])
        gt = sl_G[par2].dma(
            "sync", AR_D[:, 4096 + par2 * 2048: 4096 + (par2 + 1) * 2048],
            g_s[hh2, :, :])
        kv_t[hh2] = (kt, sl_V[par2].sem.n, gt)

    emit_head_loads(0)
    emit_head_loads(1)
    for h in range(NH):
        par = h % 2
        k_t, v_t, g_t = kv_t[h]
        Kh = AR_W[:, par * 4096:(par + 1) * 4096]
        Vh = AR_W[:, 8192 + par * 4096: 8192 + (par + 1) * 4096]
        Gh = AR_D[:, 4096 + par * 2048: 4096 + (par + 1) * 2048]
        R.wait("tensor", sl_K[par].sem, k_t)
        R.wait("tensor", sl_V[par].sem, v_t)
        exp_done = {}
        av_done = {}
        mask_done = {}
        all_b = []
        for qt in range(4):
            nk = 20 + 4 * qt
            bs = []
            for lo in range(0, 16, 3):
                bs.append((lo, min(lo + 3, 16), 0))
            for lo in range(16, nk, 3):
                bs.append((lo, min(lo + 3, nk), 1))
            for bj, (cs, ce, reg) in enumerate(bs):
                all_b.append((qt, nk, cs, ce, reg, bj == 0,
                              bj == len(bs) - 1))

        def emit_av(gi):
            qt, nk, cs, ce, reg, first, last = all_b[gi]
            Ebase = (gi % 2) * 1536
            R.wait("tensor", sEX, exp_done[gi])
            if gi in mask_done:
                R.wait("tensor", sEV, mask_done[gi])
            if first and pe3[0] is not None:
                R.wait("tensor", sEX if False else sEV, pe3[0])
            for ci in range(ce - cs):
                c = cs + ci
                Esl = AR_D[:, Ebase + ci * 512:Ebase + (ci + 1) * 512]
                R.mm("tensor", PS(6), Vh[:, c * 128:(c + 1) * 128], Esl,
                     start=(c == 0), stop=(c == nk - 1))
                av_done[gi] = R.mm(
                    "tensor", PS(7), ONES[:, :], Esl,
                    start=(c == 0), stop=(c == nk - 1),
                    sem=sMM if ci == ce - cs - 1 else None)
            if last:
                last_av = av_done[gi]
                R.wait("scalar", sMM, last_av)
                rsf = AR_F[:, 4096:4608]
                R.op("scalar", "activation", rsf, PS(7), AF.Ln, sem=sEX)
                R.wait("scalar", sEX, sEX.n)
                R.op("scalar", "activation", rsf, rsf, AF.Exp, scale=-1.0,
                     sem=sEX)
                rs_t = sEX.n
                R.wait("vector", sEX, rs_t)
                R.wait("vector", sMM, last_av)
                R.wait("vector", sl_G[par].sem, g_t)
                avn = AR_F[:, 4608:5120]
                R.op("vector", "tensor_tensor", avn, PS(6), rsf,
                     op=ALU.mult, sem=sEV)
                R.wait("vector", sEV, sEV.n)
                R.op("vector", "tensor_tensor",
                     AR_GAV[:, h * 2048 + qt * 512:
                            h * 2048 + (qt + 1) * 512],
                     avn, Gh[:, qt * 512:(qt + 1) * 512], op=ALU.mult,
                     sem=sEV)
                pe3[0] = sEV.n

        for gi, (qt, nk, cs, ce, reg, first, last) in enumerate(all_b):
            blen = ce - cs
            sh = (gi % 2) * 3
            Ebase = (gi % 2) * 1536
            if gi >= 2:
                R.wait("tensor", sEX, exp_done[gi - 2])
            for ci in range(blen):
                R.mm("tensor", PS(sh + ci),
                     Kh[:, (cs + ci) * 128:(cs + ci + 1) * 128],
                     AR_ACT[:, h * 2048 + qt * 512:
                            h * 2048 + (qt + 1) * 512],
                     start=True, stop=True,
                     sem=sMM if ci == blen - 1 else None)
            qk_t = sMM.n
            R.wait("scalar", sMM, qk_t)
            if gi >= 2:
                R.wait("scalar", sMM, av_done[gi - 2])
            R.op("scalar", "activation",
                 AR_D[:, Ebase:Ebase + blen * 512], PS(sh, blen),
                 AF.Exp, scale=SCALE_QK, bias=KB[:, reg:reg + 1], sem=sEX)
            exp_done[gi] = sEX.n
            need_mask = [ci for ci in range(blen) if cs + ci >= nk - 4]
            if need_mask:
                R.wait("vector", sEX, exp_done[gi])
                for ci in need_mask:
                    j = cs + ci - (nk - 4)
                    sl = AR_D[:, Ebase + ci * 512:Ebase + (ci + 1) * 512]
                    R.op("vector", "tensor_tensor", sl, sl,
                         MASKS[:, j * 512:(j + 1) * 512],
                         op=ALU.mult, sem=sEV)
                mask_done[gi] = sEV.n
            if gi >= 1:
                emit_av(gi - 1)
        emit_av(len(all_b) - 1)
        head_mm.append(sMM.n)
        head_ev.append(sEV.n)
        if h == 2:
            st_wt[0].wait("gpsimd")
            st_wt[1].wait("gpsimd")
            R.wait("gpsimd", sMM, head_mm[1])
            emit_gather(wqgT_loc.opt(), wqgT_all.opt(), ccg, ALL8)
        if h == 7:
            R.wait("gpsimd", sMM, head_mm[6])
            emit_gather(wqpT_loc.opt(), wqpT_all.opt(), ccp, ALL8)
        if h + 2 < NH:
            emit_head_loads(h + 2)
    barrier()
    if upto < 6:
        return finish(R, nc, es)

    o_evac = []
    o_mm = []
    p4_at = [AR_D[:, 2048:4096], AR_D[:, 4096:6144], AR_D[:, 6144:8192]]
    p4_xt = [AR_F[:, 0:2048], AR_F[:, 2048:4096]]
    p4_sq = AR_D[:, 0:2048]
    x1ev_hist = {}
    x1st_hist = {}
    ev2_hist = {}
    ln2_tmm = {}
    ln2c_ev = {}

    def ln_stats2(src, c0, sem_eng_eps, sq=None):
        sqd = p4_sq if sq is None else sq
        s1 = STAT[:, c0:c0 + 1]
        s2 = STAT[:, c0 + 1:c0 + 2]
        negm = STAT[:, c0 + 2:c0 + 3]
        m2t = STAT[:, c0 + 3:c0 + 4]
        varr = STAT[:, c0 + 4:c0 + 5]
        rstd = STAT[:, c0 + 5:c0 + 6]
        R.op("vector", "scalar_tensor_tensor", sqd, src, 1.0, src,
             op0=ALU.mult, op1=ALU.mult, accum_out=s2, sem=sEV)
        R.op("vector", "reduce_sum", s1, src, axis=AX.X, sem=sEV)
        R.wait("vector", sEV, sEV.n)
        R.op("vector", "tensor_scalar", negm, s1, -1.0 / D, None,
             op0=ALU.mult, sem=sEV)
        R.wait("vector", sEV, sEV.n)
        R.op("vector", "tensor_scalar", m2t, negm, negm, None,
             op0=ALU.mult, sem=sEV)
        R.wait("vector", sEV, sEV.n)
        R.op("vector", "scalar_tensor_tensor", varr, s2, 1.0 / D, m2t,
             op0=ALU.mult, op1=ALU.subtract, sem=sEV)
        R.wait("scalar", sEV, sEV.n)
        R.op("scalar", "activation", rstd, varr, AF.Ln, bias=EPSL[:],
             sem=sEX)
        R.wait("scalar", sEX, sEX.n)
        R.op("scalar", "activation", rstd, rstd, AF.Exp, scale=-0.5, sem=sEX)
        return sEX.n

    LN2V = AR_ACT.rearrange("p (k c) -> p k c", c=2048)

    def emit_ln2T(qb):
        par = qb % 2
        PT = psum[:, 2048 + par * 1024:
                  2048 + (par + 1) * 1024].bitcast(BF16)
        if qb >= 2:
            R.wait("tensor", sEX, ln2c_ev[qb - 2])
        R.wait("tensor", sEV, ev2_hist[qb])
        src_at = p4_at[qb % 3]
        for k in range(16):
            R.op("tensor", "transpose", PT[:, k * 128:(k + 1) * 128],
                 src_at[:, k * 128:(k + 1) * 128], IDN[:],
                 sem=sMM if k == 15 else None)
        ln2_tmm[qb] = sMM.n
        R.wait("scalar", sMM, ln2_tmm[qb])
        R.op("scalar", "activation",
             LN2V[:, :, qb * 128:(qb + 1) * 128],
             PT.rearrange("p (k c) -> p k c", c=128), AF.Copy,
             sem=sEX)
        ln2c_ev[qb] = sEX.n

    def p4b_tile(qb):
        par = qb % 2
        at = p4_at[qb % 3]
        xt = p4_xt[par]
        c0 = 16 + par * 16
        c1 = c0 + 8
        if qb >= 2:
            emit_ln2T(qb - 2)
        if qb >= 3:
            R.wait("scalar", sMM, ln2_tmm[qb - 3])
        for sl4 in st_e:
            sl4.wait("scalar")
        a_t = sl_xa.dma("scalar", at, attn_s[qb * 128:(qb + 1) * 128, :])
        if qb >= 2:
            R.wait("scalar", sEV, ev2_hist[qb - 2])
            R.wait("scalar", st_x1[par].sem, x1st_hist[qb - 2])
        x_t = sl_xb.dma("scalar", xt,
                        x_in[2048 + qb * 128: 2048 + (qb + 1) * 128, :])
        R.wait("vector", sl_xa.sem, a_t)
        rt1 = ln_stats2(at, c0, None)
        rstd1 = STAT[:, c0 + 5:c0 + 6]
        negm1 = STAT[:, c0 + 2:c0 + 3]
        nmr = STAT[:, c0 + 6:c0 + 7]
        R.wait("vector", sEX, rt1)
        R.op("vector", "tensor_scalar", nmr, negm1, rstd1, None,
             op0=ALU.mult, sem=sEV)
        R.wait("vector", sEV, sEV.n)
        R.wait("vector", sl_xb.sem, x_t)
        R.op("vector", "tensor_scalar", xt, xt, nmr, None,
             op0=ALU.add, sem=sEV)
        R.wait("vector", sEV, sEV.n)
        R.op("vector", "scalar_tensor_tensor", xt, at, rstd1, xt,
             op0=ALU.mult, op1=ALU.add, sem=sEV)
        x1ev_hist[qb] = sEV.n
        R.wait("gpsimd", sEV, x1ev_hist[qb])
        x1st_hist[qb] = st_x1[par].dma("gpsimd",
                                       x1_s[qb * 128:(qb + 1) * 128, :], xt)
        R.wait("vector", sEV, x1ev_hist[qb])
        rt2 = ln_stats2(xt, c1, None)
        R.wait("vector", sEX, rt2)
        R.op("vector", "tensor_scalar", at, xt,
             STAT[:, c1 + 2:c1 + 3], STAT[:, c1 + 5:c1 + 6],
             op0=ALU.add, op1=ALU.mult, sem=sEV)
        ev2_hist[qb] = sEV.n

    for eoh in range(2):
        if eoh == 1:
            R.wait("sync", sMM, o_mm[-1])
        ot_last = 0
        for e in range(16):
            ot_last = sl_wh.dma(
                "sync", AR_W[:, e * 1024:(e + 1) * 1024],
                wTo_s[e * 128:(e + 1) * 128,
                      eoh * 1024:(eoh + 1) * 1024])
        R.wait("tensor", sl_wh.sem, ot_last)
        for qb in range(16):
            for eot in range(2):
                idx = len(o_evac)
                if idx >= 4:
                    R.wait("tensor", sEV, o_evac[idx - 4])
                pb = PS(idx % 4)
                for e in range(16):
                    R.mm("tensor", pb,
                         AR_GAV[:, e * 2048 + qb * 128:
                                e * 2048 + (qb + 1) * 128],
                         AR_W[:, e * 1024 + eot * 512:
                              e * 1024 + (eot + 1) * 512],
                         start=(e == 0), stop=(e == 15),
                         sem=sMM if e == 15 else None)
                o_mm.append(sMM.n)
                R.wait("vector", sMM, o_mm[-1])
                sl = idx % 4
                stg = AR_E[:, sl * 512:(sl + 1) * 512]
                st_e[sl].wait("vector")
                R.op("vector", "tensor_copy", stg, pb, sem=sEV)
                o_evac.append(sEV.n)
                R.wait("sync", sEV, o_evac[-1])
                st_e[sl].dma("sync",
                             attn_s[qb * 128:(qb + 1) * 128,
                                    eoh * 1024 + eot * 512:
                                    eoh * 1024 + (eot + 1) * 512], stg)
            if eoh == 1:
                p4b_tile(qb)
    emit_ln2T(14)
    emit_ln2T(15)
    if upto < 8:
        barrier()
        return finish(R, nc, es)

    hs_pool = []
    h_evac = []
    gu_mm = []
    gu_mm_pr = {}
    wq2_t = {}
    R.wait("sync", sMM, o_mm[-1])
    R.wait("sync", ccg, 1)
    R.wait("scalar", sEX, ln2c_ev[15])
    R.wait("gpsimd", sMM, ln2_tmm[15])

    def load_gu(pr):
        par = pr % 2
        rg, ru = pr // 4, 4 + pr // 4
        hl0 = (pr % 4) * 512
        if pr >= 2:
            R.wait("sync", sMM, gu_mm_pr[pr - 2])
        for kh in range(2):
            sl_wq2[par].dma(
                "sync",
                AR_GAV[:, par * 16384 + kh * 4096:
                       par * 16384 + (kh + 1) * 4096]
                .rearrange("p (k f) -> p k f", f=512),
                wqgT_all[rg * 2048 + kh * 1024: rg * 2048 + (kh + 1) * 1024,
                         hl0:hl0 + 512]
                .rearrange("(k p) f -> p k f", p=128))
        for kh in range(2):
            wq2_t[pr] = sl_wq2[par].dma(
                "sync",
                AR_GAV[:, par * 16384 + 8192 + kh * 4096:
                       par * 16384 + 8192 + (kh + 1) * 4096]
                .rearrange("p (k f) -> p k f", f=512),
                wqgT_all[ru * 2048 + kh * 1024: ru * 2048 + (kh + 1) * 1024,
                         hl0:hl0 + 512]
                .rearrange("(k p) f -> p k f", p=128))

    def emit_ones(j):
        hb63 = (j // 4 == 63)
        h2j = AR_D[:, 2048 + (j % 2) * 512: 2048 + (j % 2) * 512 + 512]
        R.wait("tensor", sPO, hs_pool[j])
        R.mm("tensor", PS(4 + j % 4), ONES[:, :], h2j,
             start=(j // 4 == 0), stop=hb63, sem=sMM if hb63 else None)

    load_gu(0)
    load_gu(1)
    R.wait("tensor", sEV, o_evac[-1])
    for pr in range(16):
        par = pr % 2
        R.wait("tensor", sl_wq2[pr % 2].sem, wq2_t[pr])
        for hb in range(4):
            hblk = pr * 4 + hb
            for tt in range(4):
                it = hblk * 4 + tt
                ba = (it % 2) * 2
                if pr == 0 and hb == 0:
                    R.wait("tensor", sEX, ln2c_ev[(tt + 1) * 4 - 1])
                if it >= 2:
                    R.wait("tensor", sEV, h_evac[it - 2])
                for k in range(16):
                    R.mm("tensor", PS(ba),
                         AR_GAV[:, par * 16384 + k * 512 + hb * 128:
                                par * 16384 + k * 512 + (hb + 1) * 128],
                         AR_ACT[:, k * 2048 + tt * 512:
                                k * 2048 + (tt + 1) * 512],
                         start=(k == 0), stop=(k == 15))
                for k in range(16):
                    R.mm("tensor", PS(ba + 1),
                         AR_GAV[:, par * 16384 + 8192 + k * 512 + hb * 128:
                                par * 16384 + 8192 + k * 512 + (hb + 1) * 128],
                         AR_ACT[:, k * 2048 + tt * 512:
                                k * 2048 + (tt + 1) * 512],
                         start=(k == 0), stop=(k == 15),
                         sem=sMM if k == 15 else None)
                gu_mm.append(sMM.n)
                if it >= 1:
                    emit_ones(it - 1)
                R.wait("scalar", sMM, gu_mm[-1])
                sg = AR_D[:, (it % 2) * 1024:(it % 2) * 1024 + 512]
                ht = AR_D[:, (it % 2) * 1024 + 512:(it % 2) * 1024 + 1024]
                h2 = AR_D[:, 2048 + (it % 2) * 512:
                          2048 + (it % 2) * 512 + 512]
                if it >= 2:
                    R.wait("scalar", sEV, h_evac[it - 2])
                R.op("scalar", "activation", sg, PS(ba), AF.Silu, sem=sEX)
                R.wait("vector", sEX, sEX.n)
                if it >= 2:
                    R.wait("vector", sPO, hs_pool[it - 2])
                    st_h[it % 2].wait("vector")
                R.op("vector", "tensor_tensor", ht, sg, PS(ba + 1),
                     op=ALU.mult, sem=sEV)
                h_evac.append(sEV.n)
                R.wait("gpsimd", sEV, sEV.n)
                R.op("gpsimd", "tensor_tensor", h2, ht, ht, op=ALU.mult,
                     sem=sPO)
                hs_pool.append(sPO.n)
                R.wait("sync", sEV, h_evac[-1])
                st_h[it % 2].dma("sync",
                                 h_s[hblk * 128:(hblk + 1) * 128,
                                     tt * 512:(tt + 1) * 512], ht)
        gu_mm_pr[pr] = sMM.n
        if pr + 2 < 16:
            load_gu(pr + 2)
    emit_ones(255)
    rms_mm = sMM.n
    barrier()
    if upto < 9:
        return finish(R, nc, es)

    R.wait("scalar", sMM, rms_mm)
    for tt in range(4):
        R.op("scalar", "activation", RRMS[:, tt * 512:(tt + 1) * 512],
             PS(4 + tt), AF.Ln, scale=1.0 / HH, bias=EPSR[:], sem=sEX)
        R.wait("scalar", sEX, sEX.n)
        R.op("scalar", "activation", RRMS[:, tt * 512:(tt + 1) * 512],
             RRMS[:, tt * 512:(tt + 1) * 512], AF.Exp, scale=-0.5, sem=sEX)
    rrms_t = sEX.n
    R.wait("vector", sEX, rrms_t)
    for sh2 in st_h:
        sh2.wait("sync")

    f_evac = []
    p_mm = []
    mid_mm = {}
    end_mm = {}
    hx_t = {}
    wp_t = {}

    def load_wp(g):
        r = g % 8
        for half in range(2):
            if g >= 1:
                R.wait("scalar", sMM,
                       mid_mm[g - 1] if half == 0 else end_mm[g - 1])
            for qq in range(2):
                o0 = half * 8192 + qq * 4096
                s0 = r * HH + half * 4096 + qq * 2048
                wp_t[(g, half)] = sl_wp[half].dma(
                    "scalar",
                    AR_W[:, o0:o0 + 4096]
                    .rearrange("p (hc e) -> p hc e", e=256),
                    wqpT_all[s0:s0 + 2048, 0:256]
                    .rearrange("(hc p) e -> p hc e", p=128))

    R.wait("tensor", sEX, rrms_t)
    R.wait("scalar", ccp, 1)
    p7_ev = {}
    p7_st = {}
    P7FT = [AR_D[:, 2048:4096], AR_D[:, 4096:6144], AR_D[:, 6144:8192],
            AR_E[:, 0:2048]]
    P7XT = AR_F[:, 2048:4096]
    P7SQ = AR_F[:, 4096:5120].bitcast(BF16)
    f0_tgt = {}

    P7SL = [sl_wc[0], sl_wc[1], sl_qs[0], sl_qs[1]]

    def p7_early(qb):
        ft = P7FT[qb]
        c0 = 48
        for sl7 in range(4):
            st_f[sl7].wait("sync", f0_tgt[sl7])
        f_t = P7SL[qb].dma("sync", ft, f_s[:, qb * 128:(qb + 1) * 128],
                           transpose=True)
        if qb >= 1:
            R.wait("scalar", st_out[(qb - 1) % 2].sem, p7_st[qb - 1])
        x_t = sl_xb.dma("scalar", P7XT, x1_s[qb * 128:(qb + 1) * 128, :])
        R.wait("vector", P7SL[qb].sem, f_t)
        rt = ln_stats2(ft, c0, None, sq=P7SQ)
        rstd7 = STAT[:, c0 + 5:c0 + 6]
        negm7 = STAT[:, c0 + 2:c0 + 3]
        nmr7 = STAT[:, c0 + 6:c0 + 7]
        R.wait("vector", sEX, rt)
        R.op("vector", "tensor_scalar", nmr7, negm7, rstd7, None,
             op0=ALU.mult, sem=sEV)
        R.wait("vector", sEV, sEV.n)
        R.wait("vector", sl_xb.sem, x_t)
        R.op("vector", "tensor_scalar", P7XT, P7XT, nmr7, None,
             op0=ALU.add, sem=sEV)
        R.wait("vector", sEV, sEV.n)
        R.op("vector", "scalar_tensor_tensor", P7XT, ft, rstd7, P7XT,
             op0=ALU.mult, op1=ALU.add, sem=sEV)
        p7_ev[qb] = sEV.n
        R.wait("gpsimd", sEV, p7_ev[qb])
        p7_st[qb] = st_out[qb % 2].dma(
            "gpsimd", out[qb * 128:(qb + 1) * 128, :], P7XT)

    load_wp(0)
    for ttx in range(2):
        if ttx == 1:
            R.wait("sync", sMM, end_mm[7])
        for hc in range(64):
            dstA = (AR_ACT[:, hc * 1024:(hc + 1) * 1024] if hc < 32 else
                    AR_GAV[:, (hc - 32) * 1024:(hc - 31) * 1024])
            hx_t[(ttx, hc)] = sl_hx.dma(
                "sync", dstA,
                h_s[hc * 128:(hc + 1) * 128, ttx * 1024:(ttx + 1) * 1024])
        for eo in range(4):
            for ebg in range(2):
                gidx = (ttx * 8 + eo * 2 + ebg)
                bb = (gidx % 2) * 4
                e0 = eo * 512 + ebg * 256
                if gidx >= 2:
                    R.wait("tensor", sEV, f_evac[4 * (gidx - 2) + 3])
                for hc in range(64):
                    half = hc // 32
                    if hc == 0:
                        R.wait("tensor", sl_wp[0].sem, wp_t[(gidx, 0)])
                    if hc == 32:
                        R.wait("tensor", sl_wp[1].sem, wp_t[(gidx, 1)])
                    R.wait("tensor", sl_hx.sem, hx_t[(ttx, hc)])
                    hsrc = (AR_ACT[:, hc * 1024:(hc + 1) * 1024] if hc < 32
                            else AR_GAV[:, (hc - 32) * 1024:(hc - 31) * 1024])
                    w0 = half * 8192 + (hc % 32) * 256
                    for eb_in in range(2):
                        for tt_in in range(2):
                            R.mm("tensor", PS(bb + eb_in * 2 + tt_in),
                                 AR_W[:, w0 + eb_in * 128:
                                      w0 + (eb_in + 1) * 128],
                                 hsrc[:, tt_in * 512:(tt_in + 1) * 512],
                                 start=(hc == 0), stop=(hc == 63),
                                 sem=sMM if (eb_in == 1 and tt_in == 1)
                                 else None)
                    if hc == 31:
                        mid_mm[gidx] = sMM.n
                end_mm[gidx] = sMM.n
                p_mm.append(sMM.n)
                if gidx + 1 < 16:
                    load_wp(gidx + 1)
                R.wait("vector", sMM, end_mm[gidx])
                for eb_in in range(2):
                    for tt_in in range(2):
                        sidx = len(f_evac)
                        sl = sidx % 4
                        stg = AR_D[:, sl * 512:(sl + 1) * 512]
                        st_f[sl].wait("vector")
                        R.op("vector", "tensor_tensor", stg,
                             PS(bb + eb_in * 2 + tt_in),
                             RRMS[:, (ttx * 2 + tt_in) * 512:
                                  (ttx * 2 + tt_in + 1) * 512],
                             op=ALU.mult, sem=sEV)
                        f_evac.append(sEV.n)
                        R.wait("sync", sEV, sEV.n)
                        r0 = e0 + eb_in * 128
                        st_f[sl].dma("sync",
                                     f_s[r0:r0 + 128,
                                         ttx * 1024 + tt_in * 512:
                                         ttx * 1024 + (tt_in + 1) * 512], stg)
                if ttx == 1 and eo * 2 + ebg < 4:
                    p7_early(eo * 2 + ebg)
        if ttx == 0:
            for sl7 in range(4):
                f0_tgt[sl7] = st_f[sl7].sem.n
    barrier()
    if upto < 10:
        return finish(R, nc, es)

    for sf in st_f:
        sf.wait("sync")
    st_x1[0].wait("sync")
    st_x1[1].wait("sync")
    for qb in range(4, 16):
        par = qb % 2
        ft = p4_at[par]
        xt = p4_xt[par]
        c0 = 16 + par * 16
        if qb >= 2:
            R.wait("sync", sEV, p7_ev[qb - 2])
        f_t = sl_ft.dma("sync", ft, f_s[:, qb * 128:(qb + 1) * 128],
                        transpose=True)
        if qb >= 2:
            R.wait("scalar", st_out[par].sem, p7_st[qb - 2])
        x_t = sl_xb.dma("scalar", xt, x1_s[qb * 128:(qb + 1) * 128, :])
        R.wait("vector", sl_ft.sem, f_t)
        rt = ln_stats2(ft, c0, None)
        rstd = STAT[:, c0 + 5:c0 + 6]
        negm = STAT[:, c0 + 2:c0 + 3]
        nmr = STAT[:, c0 + 6:c0 + 7]
        R.wait("vector", sEX, rt)
        R.op("vector", "tensor_scalar", nmr, negm, rstd, None,
             op0=ALU.mult, sem=sEV)
        R.wait("vector", sEV, sEV.n)
        R.wait("vector", sl_xb.sem, x_t)
        R.op("vector", "tensor_scalar", xt, xt, nmr, None,
             op0=ALU.add, sem=sEV)
        R.wait("vector", sEV, sEV.n)
        R.op("vector", "scalar_tensor_tensor", xt, ft, rstd, xt,
             op0=ALU.mult, op1=ALU.add, sem=sEV)
        p7_ev[qb] = sEV.n
        R.wait("gpsimd", sEV, p7_ev[qb])
        p7_st[qb] = st_out[par].dma("gpsimd",
                               out[qb * 128:(qb + 1) * 128, :], xt)
    st_out[0].wait("gpsimd")
    st_out[1].wait("gpsimd")
    barrier()
    st_out[0].wait("sync")
    st_out[1].wait("sync")
    return finish(R, nc, es)


def make_inputs(core, full):
    seq, half = core // 2, core % 2
    x = np.asarray(full["L0_attn_x"], dtype=np.float32)
    ctx = x[seq, 0:2048]
    own = x[seq, half * 2048:(half + 1) * 2048]
    x_core = np.ascontiguousarray(np.concatenate([ctx, own], axis=0))
    kb = np.empty((128, 2), np.float32)
    kb[:, 0] = (MSHIFT if half == 1 else -100000.0)
    kb[:, 1] = MSHIFT
    return {
        "x_in": x_core,
        "kb_in": kb,
        "gb_in": np.ascontiguousarray(np.asarray(full["L0_attn_gate_b"],
                                                 np.float32)),
        "q_w": np.ascontiguousarray(np.asarray(full["L0_attn_q_w"], np.float32)),
        "k_w": np.ascontiguousarray(np.asarray(full["L0_attn_k_w"], np.float32)),
        "v_w": np.ascontiguousarray(np.asarray(full["L0_attn_v_w"], np.float32)),
        "o_w": np.ascontiguousarray(np.asarray(full["L0_attn_o_w"], np.float32)),
        "gate_w": np.ascontiguousarray(np.asarray(full["L0_attn_gate_w"],
                                                  np.float32)),
        "w_gu": np.ascontiguousarray(np.asarray(full["L0_ffn_w_gate_up"],
                                                np.float32)
                                     [core * 2048:(core + 1) * 2048]),
        "w_pr": np.ascontiguousarray(np.asarray(full["L0_ffn_w_proj"],
                                                np.float32)
                                     [core * 256:(core + 1) * 256]),
    }


def kernel(**inputs):
    nc = build_program()
    in_maps = [make_inputs(c, inputs) for c in range(8)]
    res = run_bass_kernel_spmd(nc, in_maps, core_ids=list(range(8)))
    outp = np.empty((4, 4096, 2048), np.float32)
    for c in range(8):
        seq, half = c // 2, c % 2
        outp[seq, half * 2048:(half + 1) * 2048] = res.results[c]["out"]
    return outp
